# version 05
# baseline (speedup 1.0000x reference)
import numpy as np
from contextlib import ExitStack

import concourse.bass as bass
import concourse.mybir as mybir
from concourse.bass_utils import run_bass_kernel_spmd

F32 = mybir.dt.float32
BF16 = mybir.dt.bfloat16
I32 = mybir.dt.int32
U8 = mybir.dt.uint8
AF = mybir.ActivationFunctionType
ALU = mybir.AluOpType
AX = mybir.AxisListType

BATCH, S, H, DI, NS, KC, R, FF, E = 4, 2048, 2048, 4096, 16, 4, 128, 5632, 64
DH = DI // 2
TOK = S // 2
HT = H // 128
DT = DH // 128
FT = FF // 128
KT = DI // 128
TCH = 512
NCH = S // TCH
GROUPS = [[0, 1], [2, 3], [4, 5], [6, 7]]
EPS_LN = 1e-5
EPS_RMS = 1e-6
KB = 1024

ENGS = ("sync", "vector", "scalar", "tensor", "gpsimd")


class Sem:
    def __init__(self, nc, es, name):
        self.h = es.enter_context(nc.semaphore(name))
        self.n = 0


def build_program():
    nc = bass.Bass("TRN2", target_bir_lowering=False)

    d_xb = nc.dram_tensor("x_b", [S, H], F32, kind="ExternalInput").ap()
    d_xown = nc.dram_tensor("x_own", [TOK, H], F32, kind="ExternalInput").ap()
    d_par = nc.dram_tensor("parity", [1, 1], I32, kind="ExternalInput").ap()
    d_winu = nc.dram_tensor("w_inT_u", [128, DT * HT * 128], BF16, kind="ExternalInput").ap()
    d_wing = nc.dram_tensor("w_inT_g", [128, DT * HT * 128], BF16, kind="ExternalInput").ap()
    d_lnw = nc.dram_tensor("ln_w", [128, HT], F32, kind="ExternalInput").ap()
    d_cw = nc.dram_tensor("conv_w", [128, DT * KC], F32, kind="ExternalInput").ap()
    d_cb = nc.dram_tensor("conv_b", [128, DT], F32, kind="ExternalInput").ap()
    d_wx = nc.dram_tensor("w_xT", [128, DT * 160], F32, kind="ExternalInput").ap()
    d_dtln = nc.dram_tensor("dt_ln_w", [128, 1], F32, kind="ExternalInput").ap()
    d_bln = nc.dram_tensor("b_ln_w", [16, 1], F32, kind="ExternalInput").ap()
    d_cln = nc.dram_tensor("c_ln_w", [16, 1], F32, kind="ExternalInput").ap()
    d_wdt = nc.dram_tensor("w_dtT", [R, DH], F32, kind="ExternalInput").ap()
    d_bdt = nc.dram_tensor("b_dt", [128, DT], F32, kind="ExternalInput").ap()
    d_alog = nc.dram_tensor("A_log", [128, DT * NS], F32, kind="ExternalInput").ap()
    d_dsk = nc.dram_tensor("D_skip", [128, DT], F32, kind="ExternalInput").ap()
    d_wout = nc.dram_tensor("w_outT", [128, HT * KT * 128], BF16, kind="ExternalInput").ap()
    d_ffln = nc.dram_tensor("ff_ln_w", [128, HT], F32, kind="ExternalInput").ap()
    d_wg = nc.dram_tensor("w_gT", [128, FT * HT * 128], BF16, kind="ExternalInput").ap()
    d_wu = nc.dram_tensor("w_uT", [128, FT * HT * 128], BF16, kind="ExternalInput").ap()
    d_wd = nc.dram_tensor("w_dT", [128, HT * FT * 128], BF16, kind="ExternalInput").ap()
    d_rw = nc.dram_tensor("r_wT", [128, HT * E], F32, kind="ExternalInput").ap()
    d_rb = nc.dram_tensor("r_b", [E, 1], F32, kind="ExternalInput").ap()
    d_eb = nc.dram_tensor("e_bias", [E, H], F32, kind="ExternalInput").ap()
    d_out = nc.dram_tensor("out", [TOK, H], F32, kind="ExternalOutput").ap()

    d_uf = nc.dram_tensor("u_f32", [DH, S], F32, kind="Internal").ap()
    d_ub = nc.dram_tensor("u_bf", [DH, S], BF16, kind="Internal").ap()
    d_sg = nc.dram_tensor("sg_f32", [DH, S], F32, kind="Internal").ap()
    d_xnf = nc.dram_tensor("xnT_f32", [H, TOK], F32, kind="Internal").ap()
    d_xnb = nc.dram_tensor("xnT_bf", [H, TOK], BF16, kind="Internal").ap()
    d_hid = nc.dram_tensor("hid_f32", [H, TOK], F32, kind="Internal").ap()
    d_ari = [nc.dram_tensor(f"ar_in{j}", [160, TCH], F32, kind="Internal").ap()
             for j in range(NCH)]
    d_aro = [nc.dram_tensor(f"ar_out{j}", [160, TCH], F32, kind="Internal").ap()
             for j in range(NCH)]
    d_agi = nc.dram_tensor("ag_in", [DH, S], BF16, kind="Internal").ap()
    d_ago = nc.dram_tensor("ag_out", [DI, S], BF16, kind="Internal").ap()

    es = ExitStack()
    rec = {e: [] for e in ENGS}

    def EMIT(eng, fn):
        rec[eng].append(fn)

    with es:
        AR_BYTES = 160 * KB
        arena_h = es.enter_context(nc.sbuf_tensor("arena", [128, AR_BYTES], U8))

        def V(off_b, elems, dtype):
            sz = elems * mybir.dt.size(dtype)
            assert off_b % 32 == 0 and off_b + sz <= AR_BYTES, (off_b, sz)
            return arena_h[:, off_b:off_b + sz].bitcast(dtype)

        def sb(name, shape, dtype):
            return es.enter_context(nc.sbuf_tensor(name, shape, dtype))

        t_lnw = sb("lnw", [128, HT], F32)
        t_cw = sb("cw", [128, DT * KC], F32)
        t_cb = sb("cb", [128, DT], F32)
        t_wx = sb("wx", [128, DT * 160], BF16)
        t_dtln = sb("dtln", [128, 1], F32)
        t_bln = sb("bln", [16, 1], F32)
        t_cln = sb("cln", [16, 1], F32)
        t_wdt = sb("wdt", [R, DH], BF16)
        t_bdt = sb("bdt", [128, DT], F32)
        t_aneg = sb("aneg", [128, DT * NS], F32)
        t_dsk = sb("dsk", [128, DT], F32)
        t_ffln = sb("ffln", [128, HT], F32)
        t_rb = sb("rb", [E, 1], F32)
        t_eb = sb("eb", [E, H], BF16)
        t_hst = sb("hstate", [128, DT * NS], F32)
        t_id32 = sb("ident", [128, 128], F32)
        t_idb = sb("identb", [128, 128], BF16)
        t_ones = sb("ones", [128, 128], F32)
        t_oneh = sb("onehot", [16, 16 * 128], F32)
        t_parb = sb("parb", [1, 1], I32)
        t_eps = sb("eps", [128, 2], F32)
        t_sc = sb("scratch_sc", [128, 32], F32)

        pA = es.enter_context(nc.psum_tensor("pA", [128, 2048], F32))
        pB = es.enter_context(nc.psum_tensor("pB", [128, 2048], F32))
        pA_b3_bf = pA[:, 1536:2048].bitcast(BF16)

        s_cc = Sem(nc, es, "s_cc")
        s_st = Sem(nc, es, "s_st")
        s_pe = Sem(nc, es, "s_pe")
        s_dv = Sem(nc, es, "s_dv")
        s_ac = Sem(nc, es, "s_ac")
        s_gp = Sem(nc, es, "s_gp")
        s_lw = Sem(nc, es, "s_lw")
        s_gw = Sem(nc, es, "s_gw")
        s_ar = Sem(nc, es, "s_ar")
        s_yg = Sem(nc, es, "s_yg")
        s_hid = Sem(nc, es, "s_hid")
        s_spA = Sem(nc, es, "s_spA")
        s_uf = Sem(nc, es, "s_uf")
        s_ub = Sem(nc, es, "s_ub")
        s_sg = Sem(nc, es, "s_sg")
        s_a = [Sem(nc, es, "s_a0"), Sem(nc, es, "s_a1")]
        s_b = [Sem(nc, es, "s_b0"), Sem(nc, es, "s_b1")]
        s_c = [Sem(nc, es, "s_c0"), Sem(nc, es, "s_c1")]
        s_w = [Sem(nc, es, "s_w0"), Sem(nc, es, "s_w1")]

        par_reg = es.enter_context(nc.gpsimd.register("par_reg"))

        def W(eng, sem, val=None):
            v = sem.n if val is None else val
            if v > 0:
                EMIT(eng, lambda e, s=sem.h, vv=v: e.wait_ge(s, vv))

        def OP(eng, maker, sem=None, k=1):
            if sem is None:
                EMIT(eng, maker)
                return None
            sem.n += k
            EMIT(eng, lambda e, m=maker, s=sem.h, kk=k: m(e).then_inc(s, kk))
            return sem.n

        def OPW(eng, maker, sem):
            n = OP(eng, maker, sem)
            W(eng, sem, n)
            return n

        def DMA(eng, out_ap, in_ap, sem):
            sem.n += 16
            EMIT(eng, lambda e, o=out_ap, i=in_ap, s=sem.h:
                 e.dma_start(out=o, in_=i).then_inc(s, 16))
            return sem.n

        def CC(kind, op, ins, outs):
            s_cc.n += 1
            EMIT("gpsimd", lambda g, kk=kind, oo=op, ii=ins, uu=outs:
                 g.collective_compute(kk, oo, replica_groups=GROUPS,
                                      ins=ii, outs=uu).then_inc(s_cc.h, 1))
            W("gpsimd", s_cc)

        for ap_sb, ap_dr in [(t_lnw[:], d_lnw), (t_cw[:], d_cw), (t_cb[:], d_cb),
                             (t_dtln[:], d_dtln), (t_bln[:], d_bln), (t_cln[:], d_cln),
                             (t_bdt[:], d_bdt), (t_dsk[:], d_dsk), (t_ffln[:], d_ffln),
                             (t_rb[:], d_rb), (t_parb[:], d_par)]:
            DMA("sync", ap_sb, ap_dr, s_lw)
        DMA("sync", t_aneg[:], d_alog, s_lw)
        ld_alog = s_lw.n
        DMA("gpsimd", t_wx[:], d_wx, s_gw)
        DMA("gpsimd", t_wdt[:], d_wdt, s_gw)
        DMA("gpsimd", t_eb[:], d_eb, s_gw)
        W("gpsimd", s_lw, ld_alog)
        OP("gpsimd", lambda g: g.reg_load(par_reg, t_parb[:1, :1]))

        OP("vector", lambda v: v.memset(t_ones[:], 1.0))
        OP("vector", lambda v: v.memset(t_eps[:, 0:1], EPS_LN))
        OP("vector", lambda v: v.memset(t_eps[:, 1:2], EPS_RMS))
        OP("vector", lambda v: v.memset(t_oneh[:], 0.0))
        OP("vector", lambda v: v.memset(t_hst[:], 0.0), s_dv)
        dv_ones = s_dv.n
        W("gpsimd", s_dv, dv_ones)
        OP("gpsimd", lambda g: g.affine_select(
            out=t_id32[:], in_=t_ones[:], pattern=[[1, 128]],
            compare_op=ALU.is_equal, fill=0.0, base=0, channel_multiplier=-1))
        for n in range(16):
            OP("gpsimd", lambda g, nn=n: g.affine_select(
                out=t_oneh[0:16, nn * 128:(nn + 1) * 128], in_=t_ones[0:16, 0:128],
                pattern=[[0, 128]], compare_op=ALU.is_equal, fill=0.0,
                base=-nn, channel_multiplier=1), s_gp if n == 15 else None)
        gp_init = s_gp.n
        W("vector", s_gp, gp_init)
        OP("vector", lambda v: v.tensor_copy(t_idb[:], t_id32[:]), s_dv)
        W("scalar", s_lw, ld_alog)
        OP("scalar", lambda s: s.activation(t_aneg[:], t_aneg[:], AF.Exp), s_ac)
        W("vector", s_ac, s_ac.n)
        OP("vector", lambda v: v.tensor_scalar_mul(t_aneg[:], t_aneg[:], -1.0), s_dv)
        dv_init = s_dv.n

        xn2T = V(0, HT * S, BF16)
        xa = [V(64 * KB, H, F32), V(72 * KB, H, F32)]
        za = [V(80 * KB, H, BF16), V(84 * KB, H, BF16)]
        zf = [V(88 * KB, H, F32), V(96 * KB, H, F32)]
        sq = V(104 * KB, H, F32)
        trf = V(112 * KB, 128, F32)
        trb = V(113 * KB, 128, BF16)
        wtile = [V(114 * KB, HT * 128, BF16), V(118 * KB, HT * 128, BF16)]
        convb = V(122 * KB, S, F32)
        ufb = V(130 * KB, S, F32)
        ubb = V(138 * KB, S, BF16)
        sgfb = V(142 * KB, S, F32)

        def ln_stats(buf_x, col):
            m = t_sc[:, col:col + 1]
            v_ = t_sc[:, col + 1:col + 2]
            r1 = t_sc[:, col + 2:col + 3]
            r12 = t_sc[:, col + 3:col + 4]
            tmp = t_sc[:, col + 4:col + 5]
            OPW("vector", lambda v: v.reduce_sum(m, buf_x, axis=AX.X), s_dv)
            OPW("vector", lambda v: v.tensor_scalar_mul(m, m, 1.0 / H), s_dv)
            W("scalar", s_dv, s_dv.n)
            OPW("scalar", lambda s: s.activation(sq, buf_x, AF.Square), s_ac)
            W("vector", s_ac, s_ac.n)
            OPW("vector", lambda v: v.reduce_sum(v_, sq, axis=AX.X), s_dv)
            OPW("vector", lambda v: v.tensor_scalar_mul(v_, v_, 1.0 / H), s_dv)
            OPW("vector", lambda v: v.tensor_mul(tmp, m, m), s_dv)
            OPW("vector", lambda v: v.tensor_sub(v_, v_, tmp), s_dv)
            W("scalar", s_dv, s_dv.n)
            OPW("scalar", lambda s: s.activation(tmp, v_, AF.Ln, bias=t_eps[:, 0:1]), s_ac)
            OPW("scalar", lambda s: s.activation(r1, tmp, AF.Exp, scale=-0.5), s_ac)
            W("vector", s_ac, s_ac.n)
            OPW("vector", lambda v: v.tensor_mul(tmp, r1, r1), s_dv)
            OPW("vector", lambda v: v.tensor_mul(tmp, v_, tmp), s_dv)
            W("scalar", s_dv, s_dv.n)
            OPW("scalar", lambda s: s.activation(tmp, tmp, AF.Ln, bias=t_eps[:, 1:2]), s_ac)
            OPW("scalar", lambda s: s.activation(tmp, tmp, AF.Exp, scale=-0.5), s_ac)
            W("vector", s_ac, s_ac.n)
            OPW("vector", lambda v: v.tensor_mul(r12, r1, tmp), s_dv)
            return m, r1, r12

        tile_done_dv = {}
        tile_done_ac = {}
        for tt in range(S // 128):
            pbuf = tt % 2
            if tt >= 2:
                W("sync", s_dv, tile_done_dv[tt - 2])
                W("sync", s_ac, tile_done_ac[tt - 2])
            DMA("sync", xa[pbuf], d_xb[tt * 128:(tt + 1) * 128, :], s_a[pbuf])
            W("vector", s_a[pbuf])
            m, r1, r12 = ln_stats(xa[pbuf], 0 if pbuf == 0 else 8)
            tile_done_ac[tt] = s_ac.n
            OP("vector", lambda v, xb=xa[pbuf], zb=za[pbuf], mm=m, rr=r12:
               v.tensor_scalar(zb, xb, mm, rr, ALU.subtract, ALU.mult), s_dv)
            tile_done_dv[tt] = s_dv.n
            dv_z = s_dv.n
            for ht in range(HT):
                W("tensor", s_dv, dv_z if ht == 0 else s_dv.n)
                OP("tensor", lambda t, zb=za[pbuf], h0=ht:
                   t.transpose(pA_b3_bf[:, 0:128], zb[:, h0 * 128:(h0 + 1) * 128],
                               t_idb[:]), s_pe)
                W("vector", s_pe, s_pe.n)
                OP("vector", lambda v, h0=ht, t0=tt:
                   v.tensor_scalar(xn2T[:, h0 * S + t0 * 128: h0 * S + (t0 + 1) * 128],
                                   pA_b3_bf[:, 0:128],
                                   t_lnw[:, h0:h0 + 1], None, ALU.mult), s_dv)
        dvA_done = s_dv.n

        for tt in range(TOK // 128):
            pbuf = tt % 2
            if tt >= 2:
                W("sync", s_dv, tile_done_dv[100 + tt - 2])
                W("sync", s_ac, tile_done_ac[100 + tt - 2])
            else:
                W("sync", s_dv, tile_done_dv[14 + tt])
                W("sync", s_ac, tile_done_ac[14 + tt])
            DMA("sync", xa[pbuf], d_xown[tt * 128:(tt + 1) * 128, :], s_a[pbuf])
            W("vector", s_a[pbuf])
            m, r1, r12 = ln_stats(xa[pbuf], 0 if pbuf == 0 else 8)
            tile_done_ac[100 + tt] = s_ac.n
            OP("vector", lambda v, xb=xa[pbuf], zb=zf[pbuf], mm=m, rr=r1:
               v.tensor_scalar(zb, xb, mm, rr, ALU.subtract, ALU.mult), s_dv)
            tile_done_dv[100 + tt] = s_dv.n
            dv_z = s_dv.n
            for ht in range(HT):
                W("tensor", s_dv, dv_z if ht == 0 else s_dv.n)
                if ht > 0 or tt > 0:
                    W("tensor", s_spA)
                OP("tensor", lambda t, zb=zf[pbuf], h0=ht:
                   t.transpose(pB[:, 1536:1664], zb[:, h0 * 128:(h0 + 1) * 128],
                               t_id32[:]), s_pe)
                W("vector", s_pe, s_pe.n)
                OP("vector", lambda v: v.tensor_copy(trf, pB[:, 1536:1664]))
                OP("vector", lambda v: v.tensor_copy(trb, pB[:, 1536:1664]), s_dv)
                W("sync", s_dv, s_dv.n)
                DMA("sync", d_xnf[ht * 128:(ht + 1) * 128, tt * 128:(tt + 1) * 128], trf, s_spA)
                DMA("sync", d_xnb[ht * 128:(ht + 1) * 128, tt * 128:(tt + 1) * 128], trb, s_spA)
        dvA2_done = s_dv.n

        winu3 = d_winu.rearrange("p (o x) -> p o x", x=HT * 128)
        wing3 = d_wing.rearrange("p (o x) -> p o x", x=HT * 128)

        pe_grp = {}
        psum_free = {}
        ld_ufsp = {}
        ld_ubsp = {}
        ld_sgsp = {}
        dv_ubcp = {}
        ac_silu = {}

        for o in range(2 * DT):
            is_u = o < DT
            ou = o if is_u else o - DT
            src = winu3 if is_u else wing3
            wb = wtile[o % 2]
            wb3 = wb.rearrange("p (h c) -> p h c", c=128)
            if o >= 2:
                W("gpsimd", s_pe, pe_grp[o - 2])
            DMA("gpsimd", wb, src[:, ou, :], s_w[o % 2])

            pt = pA if (o % 2 == 0) else pB
            W("tensor", s_w[o % 2])
            if o == 0:
                W("tensor", s_dv, dvA_done)
            if o == 1:
                W("tensor", s_dv, dvA2_done)
            if o >= 2:
                sem, val = psum_free[o - 2]
                W("tensor", sem, val)
            for ht in range(HT):
                for tc in range(4):
                    OP("tensor", lambda t, p=pt, h0=ht, c0=tc, w3=wb3:
                       t.matmul(p[:, c0 * 512:(c0 + 1) * 512],
                                w3[:, h0, :],
                                xn2T[:, h0 * S + c0 * 512: h0 * S + (c0 + 1) * 512],
                                start=(h0 == 0), stop=(h0 == HT - 1)),
                       s_pe if (ht == HT - 1 and tc == 3) else None)
            pe_grp[o] = s_pe.n

            if is_u:
                W("vector", s_pe, pe_grp[o])
                prev = o - 1 if o >= 1 else None
                if prev is not None and prev < DT:
                    W("vector", s_ac, ac_silu[prev])
                OPW("vector", lambda v, p=pt, o0=o:
                    v.tensor_scalar(convb[:, 3:S], p[:, 0:S - 3],
                                    t_cw[:, o0 * KC:o0 * KC + 1], None, ALU.mult), s_dv)
                for k in range(1, KC):
                    OPW("vector", lambda v, p=pt, o0=o, kk=k:
                        v.scalar_tensor_tensor(convb[:, 3:S], p[:, kk:kk + S - 3],
                                               t_cw[:, o0 * KC + kk:o0 * KC + kk + 1],
                                               convb[:, 3:S], ALU.mult, ALU.add), s_dv)
                OP("vector", lambda v, p=pt, o0=o:
                   v.tensor_scalar(convb[:, 0:1], p[:, 0:1],
                                   t_cw[:, o0 * KC + 3:o0 * KC + 4], None, ALU.mult))
                OPW("vector", lambda v, p=pt, o0=o:
                    v.tensor_scalar(convb[:, 1:2], p[:, 0:1],
                                    t_cw[:, o0 * KC + 2:o0 * KC + 3], None, ALU.mult), s_dv)
                OP("vector", lambda v, p=pt, o0=o:
                   v.scalar_tensor_tensor(convb[:, 1:2], p[:, 1:2],
                                          t_cw[:, o0 * KC + 3:o0 * KC + 4],
                                          convb[:, 1:2], ALU.mult, ALU.add))
                OPW("vector", lambda v, p=pt, o0=o:
                    v.tensor_scalar(convb[:, 2:3], p[:, 0:1],
                                    t_cw[:, o0 * KC + 1:o0 * KC + 2], None, ALU.mult), s_dv)
                for k2 in (2, 3):
                    OPW("vector", lambda v, p=pt, o0=o, kk=k2:
                        v.scalar_tensor_tensor(convb[:, 2:3], p[:, kk - 1:kk],
                                               t_cw[:, o0 * KC + kk:o0 * KC + kk + 1],
                                               convb[:, 2:3], ALU.mult, ALU.add), s_dv)
                psum_free[o] = (s_dv, s_dv.n)
                W("scalar", s_dv, s_dv.n)
                if o >= 1 and (o - 1) in ld_ufsp:
                    W("scalar", s_dv, dv_ubcp[o - 1])
                    W("scalar", s_uf, ld_ufsp[o - 1])
                OP("scalar", lambda s, o0=o:
                   s.activation(ufb, convb[:, 0:S], AF.Silu,
                                bias=t_cb[:, o0:o0 + 1]), s_ac)
                ac_silu[o] = s_ac.n
                W("vector", s_ac, ac_silu[o])
                if o >= 1 and (o - 1) in ld_ubsp:
                    W("vector", s_ub, ld_ubsp[o - 1])
                OP("vector", lambda v: v.tensor_copy(ubb, ufb), s_dv)
                dv_ubcp[o] = s_dv.n
                W("sync", s_ac, ac_silu[o])
                ld_ufsp[o] = DMA("sync", d_uf[o * 128:(o + 1) * 128, :], ufb, s_uf)
                W("sync", s_dv, dv_ubcp[o])
                ld_ubsp[o] = DMA("sync", d_ub[o * 128:(o + 1) * 128, :], ubb, s_ub)
            else:
                W("scalar", s_pe, pe_grp[o])
                if (o - 1) >= DT:
                    W("scalar", s_sg, ld_sgsp[o - 1])
                OP("scalar", lambda s, p=pt: s.activation(sgfb, p[:, 0:S], AF.Silu), s_ac)
                psum_free[o] = (s_ac, s_ac.n)
                W("sync", s_ac, s_ac.n)
                ld_sgsp[o] = DMA("sync", d_sg[ou * 128:(ou + 1) * 128, :], sgfb, s_sg)
        acB_done = s_ac.n
        dvB_done = s_dv.n

        brep = V(0, NS * TCH, F32)
        crep = V(32 * KB, NS * TCH, F32)
        dab = V(64 * KB, 4 * TCH, F32)
        dbb = V(72 * KB, 4 * TCH, F32)
        htr = V(80 * KB, 4 * TCH, F32)
        dtt = [V(88 * KB, TCH, F32), V(90 * KB, TCH, F32)]
        dtu = V(92 * KB, TCH, F32)
        uft = [V(94 * KB, TCH, F32), V(96 * KB, TCH, F32)]
        ubt = [V(98 * KB, TCH, BF16), V(99 * KB, TCH, BF16)]
        sgt = [V(100 * KB, TCH, F32), V(102 * KB, TCH, F32)]
        ya = V(104 * KB, TCH, F32)
        yt1 = V(106 * KB, TCH, F32)
        yt2 = V(108 * KB, TCH, F32)
        ybf = [V(110 * KB, TCH, BF16), V(111 * KB, TCH, BF16)]
        trn = V(112 * KB, TCH, BF16)
        trf32 = V(113 * KB, TCH, F32)
        bnf = V(115 * KB, TCH, F32)
        cnf = V(117 * KB, TCH, F32)
        sqc = V(119 * KB, TCH, F32)
        arst = V(121 * KB, 3 * TCH, F32)
        rfs = V(127 * KB, TCH, F32)
        rfs2 = V(129 * KB, TCH, F32)

        wx3 = t_wx[:].rearrange("p (d o) -> p d o", o=160)
        htr3 = htr.rearrange("p (n t) -> p n t", t=TCH)

        W("tensor", s_gw, 48)
        W("vector", s_dv, dv_init)
        W("sync", s_uf)
        W("sync", s_ub)
        W("sync", s_sg)
        W("vector", s_spA)
        W("gpsimd", s_spA)

        pe_wx = {}
        ac_sp_last = acB_done
        dv_scan = {}
        ld_yst = {}
        for j in range(NCH):
            c0 = j * TCH
            for dt_i in range(DT):
                pb = dt_i % 2
                if dt_i >= 2:
                    W("sync", s_pe, pe_wx[dt_i - 2])
                DMA("sync", ubt[pb], d_ub[dt_i * 128:(dt_i + 1) * 128, c0:c0 + TCH], s_a[pb])
                W("tensor", s_a[pb])
                if dt_i == 0:
                    W("tensor", s_ac, ac_sp_last)
                    W("tensor", s_dv, s_dv.n)
                OP("tensor", lambda t, u=ubt[pb], d0=dt_i:
                   t.matmul(pA[:, 0:TCH], wx3[:, d0, 0:128], u,
                            start=(d0 == 0), stop=(d0 == DT - 1)))
                OP("tensor", lambda t, u=ubt[pb], d0=dt_i:
                   t.matmul(pA[0:16, 512:512 + TCH], wx3[:, d0, 128:144], u,
                            start=(d0 == 0), stop=(d0 == DT - 1)))
                pe_wx[dt_i] = OP("tensor", lambda t, u=ubt[pb], d0=dt_i:
                                 t.matmul(pA[0:16, 1024:1024 + TCH], wx3[:, d0, 144:160], u,
                                          start=(d0 == 0), stop=(d0 == DT - 1)), s_pe)
            W("vector", s_pe, s_pe.n)
            if j > 0:
                W("vector", s_ar)
            OP("vector", lambda v: v.tensor_copy(arst[:, 0:TCH], pA[:, 0:TCH]))
            OP("vector", lambda v: v.tensor_copy(arst[0:16, TCH:2 * TCH], pA[0:16, 512:512 + TCH]))
            OP("vector", lambda v: v.tensor_copy(arst[0:16, 2 * TCH:3 * TCH], pA[0:16, 1024:1024 + TCH]), s_dv)
            W("gpsimd", s_dv, s_dv.n)
            DMA("gpsimd", d_ari[j][0:128, :], arst[:, 0:TCH], s_ar)
            DMA("gpsimd", d_ari[j][128:144, :], arst[0:16, TCH:2 * TCH], s_ar)
            DMA("gpsimd", d_ari[j][144:160, :], arst[0:16, 2 * TCH:3 * TCH], s_ar)
            W("gpsimd", s_ar)
            CC("AllReduce", ALU.add, [d_ari[j]], [d_aro[j]])
            DMA("gpsimd", trf32, d_aro[j][0:128, :], s_ar)
            DMA("gpsimd", bnf[0:16, :], d_aro[j][128:144, :], s_ar)
            DMA("gpsimd", cnf[0:16, :], d_aro[j][144:160, :], s_ar)
            gd_ar = s_ar.n

            W("scalar", s_ar, gd_ar)
            W("scalar", s_pe, s_pe.n)
            OP("scalar", lambda s: s.activation(sqc, trf32, AF.Square), s_ac)
            W("tensor", s_ac, s_ac.n)
            W("tensor", s_dv, s_dv.n)
            OP("tensor", lambda t: t.matmul(pB[0:1, 0:TCH], t_ones[:, 0:1], sqc,
                                            start=True, stop=True), s_pe)
            W("scalar", s_pe, s_pe.n)
            OPW("scalar", lambda s: s.activation(rfs2[0:1, :], pB[0:1, 0:TCH],
                                                 AF.Ln, bias=t_eps[0:1, 1:2], scale=1.0 / R), s_ac)
            OP("scalar", lambda s: s.activation(rfs[0:1, :], rfs2[0:1, :],
                                                AF.Exp, scale=-0.5), s_ac)
            W("tensor", s_ac, s_ac.n)
            OP("tensor", lambda t: t.matmul(pB[:, 1024:1024 + TCH], t_ones[0:1, :],
                                            rfs[0:1, :], start=True, stop=True), s_pe)
            W("vector", s_pe, s_pe.n)
            OP("vector", lambda v: v.scalar_tensor_tensor(
                trn, trf32, t_dtln[:, 0:1], pB[:, 1024:1024 + TCH], ALU.mult, ALU.mult), s_dv)
            for buf, w_ap in ((bnf, t_bln), (cnf, t_cln)):
                W("scalar", s_pe, s_pe.n)
                OP("scalar", lambda s, b=buf: s.activation(sqc[0:16, :], b[0:16, :], AF.Square), s_ac)
                W("tensor", s_ac, s_ac.n)
                OP("tensor", lambda t: t.matmul(pB[0:1, 0:TCH], t_ones[0:16, 0:1], sqc[0:16, :],
                                                start=True, stop=True), s_pe)
                W("scalar", s_pe, s_pe.n)
                OPW("scalar", lambda s: s.activation(rfs2[0:1, :], pB[0:1, 0:TCH],
                                                     AF.Ln, bias=t_eps[0:1, 1:2], scale=1.0 / NS), s_ac)
                OP("scalar", lambda s: s.activation(rfs[0:1, :], rfs2[0:1, :],
                                                    AF.Exp, scale=-0.5), s_ac)
                W("tensor", s_ac, s_ac.n)
                W("tensor", s_dv, s_dv.n)
                OP("tensor", lambda t: t.matmul(pB[0:16, 1024:1024 + TCH], t_ones[0:1, 0:16],
                                                rfs[0:1, :], start=True, stop=True), s_pe)
                W("vector", s_pe, s_pe.n)
                OP("vector", lambda v, b=buf, w=w_ap: v.scalar_tensor_tensor(
                    b[0:16, :], b[0:16, :], w[0:16, 0:1], pB[0:16, 1024:1024 + TCH],
                    ALU.mult, ALU.mult), s_dv)

            for n in range(NS):
                for src_buf, dst in ((bnf, brep), (cnf, crep)):
                    W("tensor", s_dv, s_dv.n)
                    OP("tensor", lambda t, nn=n, sbuf=src_buf: t.matmul(
                        pB[:, 0:TCH], t_oneh[0:16, nn * 128:(nn + 1) * 128], sbuf[0:16, :],
                        start=True, stop=True), s_pe)
                    W("vector", s_pe, s_pe.n)
                    OP("vector", lambda v, nn=n, db=dst: v.tensor_copy(
                        db[:, nn * TCH:(nn + 1) * TCH], pB[:, 0:TCH]), s_dv)
            dv_rep = s_dv.n

            for dt_i in range(DT):
                pb = dt_i % 2
                if dt_i >= 2 or j > 0:
                    W("sync", s_dv, dv_scan[(dt_i - 2) % DT])
                    W("sync", s_c[pb], ld_yst[(dt_i - 2) % DT])
                DMA("sync", uft[pb], d_uf[dt_i * 128:(dt_i + 1) * 128, c0:c0 + TCH], s_b[pb])
                DMA("sync", sgt[pb], d_sg[dt_i * 128:(dt_i + 1) * 128, c0:c0 + TCH], s_b[pb])
                ld_u = s_b[pb].n
                W("tensor", s_dv, dv_rep if dt_i == 0 else s_dv.n)
                W("tensor", s_ac, s_ac.n)
                OP("tensor", lambda t, d0=dt_i: t.matmul(
                    pA[:, 0:TCH], t_wdt[:, d0 * 128:(d0 + 1) * 128], trn,
                    start=True, stop=True), s_pe)
                W("scalar", s_pe, s_pe.n)
                dtb = dtt[pb]
                OPW("scalar", lambda s, d0=dt_i: s.activation(
                    pA[:, 512:512 + TCH], pA[:, 0:TCH], AF.Abs, bias=t_bdt[:, d0:d0 + 1]), s_ac)
                OPW("scalar", lambda s: s.activation(
                    pA[:, 512:512 + TCH], pA[:, 512:512 + TCH], AF.Exp, scale=-1.0), s_ac)
                OP("scalar", lambda s: s.activation(
                    pA[:, 512:512 + TCH], pA[:, 512:512 + TCH], AF.Ln, bias=1.0))
                OP("scalar", lambda s, d0=dt_i, ob=dtb: s.activation(
                    ob, pA[:, 0:TCH], AF.Relu, bias=t_bdt[:, d0:d0 + 1]), s_ac)
                ac_sp_last = s_ac.n
                W("vector", s_ac, s_ac.n)
                OPW("vector", lambda v, ob=dtb: v.tensor_add(ob, ob, pA[:, 512:512 + TCH]), s_dv)
                W("vector", s_b[pb], ld_u)
                OP("vector", lambda v, ob=dtb, u=uft[pb]: v.tensor_mul(dtu, ob, u), s_dv)
                dv_dtu = s_dv.n
                W("scalar", s_dv, dv_dtu)
                for g in range(4):
                    W("scalar", s_dv, s_dv.n)
                    for ni in range(4):
                        n = g * 4 + ni
                        OP("scalar", lambda s, d0=dt_i, nn=n, ni_=ni, ob=dtb: s.activation(
                            dab[:, ni_ * TCH:(ni_ + 1) * TCH], ob, AF.Exp,
                            scale=t_aneg[:, d0 * NS + nn:d0 * NS + nn + 1]),
                           s_ac if ni == 3 else None)
                    W("vector", s_ac, s_ac.n)
                    if g == 0:
                        W("vector", s_dv, dv_dtu)
                    for ni in range(4):
                        n = g * 4 + ni
                        OP("vector", lambda v, nn=n, ni_=ni: v.tensor_mul(
                            dbb[:, ni_ * TCH:(ni_ + 1) * TCH], dtu,
                            brep[:, nn * TCH:(nn + 1) * TCH]), s_dv if ni == 3 else None)
                    W("vector", s_dv, s_dv.n)
                    for ni in range(4):
                        n = g * 4 + ni
                        OP("vector", lambda v, d0=dt_i, nn=n, ni_=ni: v.tensor_tensor_scan(
                            out=htr[:, ni_ * TCH:(ni_ + 1) * TCH],
                            data0=dab[:, ni_ * TCH:(ni_ + 1) * TCH],
                            data1=dbb[:, ni_ * TCH:(ni_ + 1) * TCH],
                            initial=t_hst[:, d0 * NS + nn:d0 * NS + nn + 1],
                            op0=ALU.mult, op1=ALU.add), s_dv if ni == 3 else None)
                    W("vector", s_dv, s_dv.n)
                    OPW("vector", lambda v, d0=dt_i, g0=g: v.tensor_copy(
                        t_hst[:, d0 * NS + g0 * 4: d0 * NS + (g0 + 1) * 4]
                        .rearrange("p (a b) -> p a b", b=1),
                        htr3[:, :, TCH - 1:TCH]), s_dv)
                    for ni in range(4):
                        n = g * 4 + ni
                        OP("vector", lambda v, nn=n, ni_=ni: v.tensor_mul(
                            dbb[:, ni_ * TCH:(ni_ + 1) * TCH],
                            htr[:, ni_ * TCH:(ni_ + 1) * TCH],
                            crep[:, nn * TCH:(nn + 1) * TCH]), s_dv if ni == 3 else None)
                    W("vector", s_dv, s_dv.n)
                    OP("vector", lambda v: v.tensor_add(yt1, dbb[:, 0:TCH], dbb[:, TCH:2 * TCH]))
                    OPW("vector", lambda v: v.tensor_add(yt2, dbb[:, 2 * TCH:3 * TCH], dbb[:, 3 * TCH:4 * TCH]), s_dv)
                    if g == 0:
                        OP("vector", lambda v: v.tensor_add(ya, yt1, yt2), s_dv)
                    else:
                        OPW("vector", lambda v: v.tensor_add(yt1, yt1, yt2), s_dv)
                        OP("vector", lambda v: v.tensor_add(ya, ya, yt1), s_dv)
                W("vector", s_dv, s_dv.n)
                OPW("vector", lambda v, d0=dt_i, u=uft[pb]: v.scalar_tensor_tensor(
                    ya, u, t_dsk[:, d0:d0 + 1], ya, ALU.mult, ALU.add), s_dv)
                OP("vector", lambda v, sgb=sgt[pb], yb=ybf[pb]: v.tensor_mul(yb, ya, sgb), s_dv)
                dv_scan[dt_i] = s_dv.n
                W("sync", s_dv, s_dv.n)
                ld_yst[dt_i] = DMA("sync", d_agi[dt_i * 128:(dt_i + 1) * 128, c0:c0 + TCH],
                                   ybf[pb], s_c[pb])

        W("gpsimd", s_c[0])
        W("gpsimd", s_c[1])
        CC("AllGather", ALU.bypass, [d_agi], [d_ago])

        yag = V(0, KT * TOK, BF16)
        hff = V(0, HT * TOK, BF16)
        sres = V(0, HT * TOK, F32)
        wdl = [V(64 * KB, KT * 128, BF16), V(72 * KB, KT * 128, BF16)]
        wgl = [V(64 * KB, HT * 128, BF16), V(68 * KB, HT * 128, BF16)]
        wul = [V(72 * KB, HT * 128, BF16), V(76 * KB, HT * 128, BF16)]
        wdn = [V(64 * KB, FT * 128, BF16), V(75 * KB, FT * 128, BF16)]
        xnt = [V(86 * KB, TOK, F32), V(90 * KB, TOK, F32)]
        hidw = V(94 * KB, TOK, F32)
        sqd = V(98 * KB, TOK, F32)
        ssum = V(102 * KB, TOK, F32)
        rfacb = V(106 * KB, TOK, F32)
        ffm = V(110 * KB, FT * 512, BF16)
        xnb2 = [V(110 * KB, TOK, BF16), V(112 * KB, TOK, BF16)]
        lg = V(114 * KB, TOK, F32)
        lgt = V(118 * KB, 64, F32)
        m8b = V(119 * KB, 8, F32)
        mskb = V(120 * KB, 64, F32)
        snb = V(121 * KB, 64, BF16)
        snT = V(122 * KB, TOK, BF16)
        swb = V(124 * KB, TOK, F32)
        rwl = V(128 * KB, HT * E, BF16)
        gtile = V(130 * KB, TOK, F32)
        outw = V(134 * KB, H, F32)
        sgate = V(154 * KB, 512, F32)
        sld = V(156 * KB, TOK, F32)

        yag3 = yag.rearrange("p (k t) -> p k t", t=TOK)
        hff3 = hff.rearrange("p (h t) -> p h t", t=TOK)
        d3_dv = {}
        sres3 = sres.rearrange("p (h t) -> p h t", t=TOK)
        ffm3 = ffm.rearrange("p (f t) -> p f t", t=512)

        s_yg.n += 16 * KT

        def _readback(g):
            with g.If_eq(par_reg, 0):
                for k in range(KT):
                    g.dma_start(out=yag3[:, k, :],
                                in_=d_ago[k * 128:(k + 1) * 128, 0:TOK]).then_inc(s_yg.h, 16)
            with g.Else():
                for k in range(KT):
                    g.dma_start(out=yag3[:, k, :],
                                in_=d_ago[k * 128:(k + 1) * 128, TOK:S]).then_inc(s_yg.h, 16)
        EMIT("gpsimd", _readback)
        gd_yag = s_yg.n

        wout3 = d_wout.rearrange("p (t x) -> p t x", x=KT * 128)
        wg3 = d_wg.rearrange("p (t x) -> p t x", x=HT * 128)
        wu3 = d_wu.rearrange("p (t x) -> p t x", x=HT * 128)
        wd3 = d_wd.rearrange("p (t x) -> p t x", x=FT * 128)

        W("sync", s_spA)
        pe_wo = {}
        dv_hid = {}
        for ho in range(HT):
            pb2 = ho % 2
            if ho >= 2:
                W("gpsimd", s_pe, pe_wo[ho - 2])
            DMA("gpsimd", wdl[pb2], wout3[:, ho, :], s_w[pb2])
            W("tensor", s_w[pb2])
            if ho == 0:
                W("tensor", s_yg, gd_yag)
                W("tensor", s_dv, s_dv.n)
                W("tensor", s_ac, s_ac.n)
            if ho >= 2:
                W("tensor", s_dv, dv_hid[ho - 2])
            wv = wdl[pb2].rearrange("p (k c) -> p k c", c=128)
            pg = pA if pb2 == 0 else pB
            for k in range(KT):
                for tc in range(2):
                    OP("tensor", lambda t, p=pg, k0=k, c0=tc, w3=wv:
                       t.matmul(p[:, c0 * 512:(c0 + 1) * 512], w3[:, k0, :],
                                yag3[:, k0, c0 * 512:(c0 + 1) * 512],
                                start=(k0 == 0), stop=(k0 == KT - 1)),
                       s_pe if (k == KT - 1 and tc == 1) else None)
            pe_wo[ho] = s_pe.n
            DMA("sync", xnt[pb2], d_xnf[ho * 128:(ho + 1) * 128, :], s_a[pb2])
            W("vector", s_pe, pe_wo[ho])
            W("vector", s_a[pb2])
            OP("vector", lambda v, p=pg, xb=xnt[pb2]:
               v.tensor_add(hidw, xb, p[:, 0:TOK]), s_dv)
            dv_hid[ho] = s_dv.n
            W("sync", s_dv, s_dv.n)
            DMA("sync", d_hid[ho * 128:(ho + 1) * 128, :], hidw, s_hid)
            W("vector", s_hid)

        W("sync", s_hid)
        d2_ac = {}
        for ho in range(HT):
            pb2 = ho % 2
            if ho >= 2:
                W("sync", s_ac, d2_ac[ho - 2])
            DMA("sync", xnt[pb2], d_hid[ho * 128:(ho + 1) * 128, :], s_a[pb2])
            W("scalar", s_a[pb2])
            if ho >= 1:
                W("scalar", s_pe, s_pe.n)
            OP("scalar", lambda s, xb=xnt[pb2]: s.activation(sqd, xb, AF.Square), s_ac)
            d2_ac[ho] = s_ac.n
            W("tensor", s_ac, s_ac.n)
            if ho == 0:
                W("tensor", s_dv, s_dv.n)
            for tc in range(2):
                OP("tensor", lambda t, c0=tc, h0=ho: t.matmul(
                    pA[0:1, c0 * 512:(c0 + 1) * 512], t_ones[:, 0:1],
                    sqd[:, c0 * 512:(c0 + 1) * 512],
                    start=(h0 == 0), stop=(h0 == HT - 1)),
                   s_pe if tc == 1 else None)
        W("scalar", s_pe, s_pe.n)
        OPW("scalar", lambda s: s.activation(ssum[0:1, :], pA[0:1, 0:TOK],
                                             AF.Ln, bias=t_eps[0:1, 1:2], scale=1.0 / H), s_ac)
        OP("scalar", lambda s: s.activation(ssum[0:1, :], ssum[0:1, :],
                                            AF.Exp, scale=-0.5), s_ac)
        W("tensor", s_ac, s_ac.n)
        for tc in range(2):
            OP("tensor", lambda t, c0=tc: t.matmul(
                pA[:, 1024 + c0 * 512:1024 + (c0 + 1) * 512], t_ones[0:1, :],
                ssum[0:1, c0 * 512:(c0 + 1) * 512], start=True, stop=True),
               s_pe if tc == 1 else None)
        W("vector", s_pe, s_pe.n)
        OP("vector", lambda v: v.tensor_copy(rfacb, pA[:, 1024:1024 + TOK]), s_dv)
        dv_rfac = s_dv.n

        W("sync", s_ac)
        for ho in range(HT):
            pb2 = ho % 2
            if ho >= 2:
                W("sync", s_dv, d3_dv[ho - 2])
            DMA("sync", xnt[pb2], d_hid[ho * 128:(ho + 1) * 128, :], s_a[pb2])
            W("vector", s_a[pb2])
            W("vector", s_dv, dv_rfac)
            OP("vector", lambda v, xb=xnt[pb2], h0=ho: v.scalar_tensor_tensor(
                hff3[:, h0, :], xb, t_ffln[:, h0:h0 + 1], rfacb, ALU.mult, ALU.mult), s_dv)
            d3_dv[ho] = s_dv.n
        dv_hff = s_dv.n

        W("sync", s_dv)
        pe_ff = {}
        dv_ffm = {}
        for th in range(2):
            tc0 = th * 512
            for f in range(FT):
                pb2 = f % 2
                key = (th, f)
                prevk = (th, f - 2) if f >= 2 else None
                if prevk in pe_ff:
                    W("gpsimd", s_pe, pe_ff[prevk])
                elif f < 2:
                    W("gpsimd", s_pe, s_pe.n)
                DMA("gpsimd", wgl[pb2], wg3[:, f, :], s_w[pb2])
                DMA("gpsimd", wul[pb2], wu3[:, f, :], s_w[pb2])
                W("tensor", s_w[pb2])
                if th == 0 and f == 0:
                    W("tensor", s_dv, dv_hff)
                if prevk in dv_ffm:
                    W("tensor", s_dv, dv_ffm[prevk])
                elif th == 1 and f < 2:
                    W("tensor", s_dv, s_dv.n)
                wgv = wgl[pb2].rearrange("p (h c) -> p h c", c=128)
                wuv = wul[pb2].rearrange("p (h c) -> p h c", c=128)
                pg = pA if pb2 == 0 else pB
                for ho in range(HT):
                    OP("tensor", lambda t, p=pg, h0=ho, w3=wgv, c0=tc0:
                       t.matmul(p[:, 0:512], w3[:, h0, :], hff3[:, h0, c0:c0 + 512],
                                start=(h0 == 0), stop=(h0 == HT - 1)))
                    OP("tensor", lambda t, p=pg, h0=ho, w3=wuv, c0=tc0:
                       t.matmul(p[:, 512:1024], w3[:, h0, :], hff3[:, h0, c0:c0 + 512],
                                start=(h0 == 0), stop=(h0 == HT - 1)),
                       s_pe if ho == HT - 1 else None)
                pe_ff[key] = s_pe.n
                W("scalar", s_pe, pe_ff[key])
                if not (th == 0 and f == 0):
                    W("scalar", s_dv, s_dv.n)
                OP("scalar", lambda s, p=pg: s.activation(sgate, p[:, 0:512], AF.Silu), s_ac)
                W("vector", s_ac, s_ac.n)
                OP("vector", lambda v, p=pg, f0=f: v.tensor_mul(
                    ffm3[:, f0, :], p[:, 512:1024], sgate), s_dv)
                dv_ffm[key] = s_dv.n
            for ho in range(HT):
                pb2 = ho % 2
                W("gpsimd", s_pe, s_pe.n)
                DMA("gpsimd", wdn[pb2], wd3[:, ho, :], s_w[pb2])
                W("tensor", s_w[pb2])
                W("tensor", s_dv, s_dv.n)
                wdv = wdn[pb2].rearrange("p (k c) -> p k c", c=128)
                pg = pA if pb2 == 0 else pB
                for k in range(FT):
                    OP("tensor", lambda t, p=pg, k0=k, w3=wdv:
                       t.matmul(p[:, 1536:2048], w3[:, k0, :], ffm3[:, k0, :],
                                start=(k0 == 0), stop=(k0 == FT - 1)),
                       s_pe if k == FT - 1 else None)
                DMA("sync", xnt[pb2][:, 0:512], d_hid[ho * 128:(ho + 1) * 128, tc0:tc0 + 512], s_a[pb2])
                W("vector", s_pe, s_pe.n)
                W("vector", s_a[pb2])
                OP("vector", lambda v, p=pg, xb=xnt[pb2]: v.tensor_add(
                    hidw[:, 0:512], xb[:, 0:512], p[:, 1536:2048]), s_dv)
                W("sync", s_dv, s_dv.n)
                DMA("sync", d_hid[ho * 128:(ho + 1) * 128, tc0:tc0 + 512], hidw[:, 0:512], s_hid)
                W("vector", s_hid)
        dv_sres = s_dv.n

        rw3v = rwl.rearrange("p (h c) -> p h c", c=E)
        W("gpsimd", s_dv, dv_ffm[(1, FT - 1)])
        W("gpsimd", s_pe, s_pe.n)
        DMA("gpsimd", rw3v, d_rw.rearrange("p (h c) -> p h c", c=E), s_gw)
        gd_rw = s_gw.n
        rt_pe = {}
        for ho in range(HT):
            pb2 = ho % 2
            if ho == 0:
                W("sync", s_pe, s_pe.n)
            if ho >= 2:
                W("sync", s_pe, rt_pe[ho - 2])
            DMA("sync", xnb2[pb2], d_xnb[ho * 128:(ho + 1) * 128, :], s_b[pb2])
            W("tensor", s_b[pb2])
            if ho == 0:
                W("tensor", s_gw, gd_rw)
                W("tensor", s_dv, dv_sres)
            for tc in range(2):
                OP("tensor", lambda t, h0=ho, c0=tc, xb=xnb2[pb2]: t.matmul(
                    pA[0:E, c0 * 512:(c0 + 1) * 512], rw3v[:, h0, :],
                    xb[:, c0 * 512:(c0 + 1) * 512],
                    start=(h0 == 0), stop=(h0 == HT - 1)),
                   s_pe if tc == 1 else None)
            rt_pe[ho] = s_pe.n
        W("scalar", s_pe, s_pe.n)
        OP("scalar", lambda s: s.activation(lg[0:E, :], pA[0:E, 0:TOK], AF.Identity,
                                            bias=t_rb[0:E, 0:1]), s_ac)
        ac_lg = s_ac.n
        for tt in range(TOK // 128):
            W("tensor", s_ac, ac_lg if tt == 0 else s_ac.n)
            W("tensor", s_dv, s_dv.n)
            OP("tensor", lambda t, t0=tt: t.transpose(
                pB[:, 0:64], lg[0:E, t0 * 128:(t0 + 1) * 128], t_id32[0:E, 0:E]), s_pe)
            W("vector", s_pe, s_pe.n)
            OPW("vector", lambda v: v.tensor_copy(lgt, pB[:, 0:64]), s_dv)
            OPW("vector", lambda v: v.reduce_max(t_sc[:, 20:21], lgt, axis=AX.X, negate=True), s_dv)
            W("scalar", s_dv, s_dv.n)
            OPW("scalar", lambda s: s.activation(lgt, lgt, AF.Exp, bias=t_sc[:, 20:21]), s_ac)
            W("vector", s_ac, s_ac.n)
            OPW("vector", lambda v: v.max(out=m8b[:, 0:8], in_=lgt), s_dv)
            OPW("vector", lambda v: v.tensor_scalar(mskb, lgt, m8b[:, 7:8], None, ALU.is_ge), s_dv)
            OPW("vector", lambda v: v.reduce_sum(t_sc[:, 21:22], lgt, axis=AX.X), s_dv)
            OPW("vector", lambda v: v.reduce_sum(t_sc[:, 22:23], m8b[:, 0:8], axis=AX.X), s_dv)
            OPW("vector", lambda v: v.reciprocal(t_sc[:, 23:24], t_sc[:, 21:22]), s_dv)
            OPW("vector", lambda v: v.tensor_mul(mskb, mskb, lgt), s_dv)
            OPW("vector", lambda v: v.tensor_scalar(snb, mskb, t_sc[:, 23:24], None, ALU.mult), s_dv)
            OPW("vector", lambda v: v.tensor_mul(t_sc[:, 24:25], t_sc[:, 22:23], t_sc[:, 23:24]), s_dv)
            W("tensor", s_dv, s_dv.n)
            OP("tensor", lambda t: t.transpose(
                pB[:, 512:640].bitcast(BF16)[0:E, 0:128], snb, t_idb[:]), s_pe)
            OP("tensor", lambda t: t.transpose(
                pB[0:1, 1024:1152], t_sc[:, 24:25], t_id32[:]), s_pe)
            W("vector", s_pe, s_pe.n)
            OP("vector", lambda v, t0=tt: v.tensor_copy(
                snT[0:E, t0 * 128:(t0 + 1) * 128], pB[:, 512:640].bitcast(BF16)[0:E, 0:128]))
            OP("vector", lambda v, t0=tt: v.tensor_copy(
                swb[0:1, t0 * 128:(t0 + 1) * 128], pB[0:1, 1024:1152]), s_dv)
        dv_sn = s_dv.n
        W("tensor", s_dv, dv_sn)
        for tc in range(2):
            OP("tensor", lambda t, c0=tc: t.matmul(
                pB[:, 1024 + c0 * 512:1024 + (c0 + 1) * 512], t_ones[0:1, :],
                swb[0:1, c0 * 512:(c0 + 1) * 512], start=True, stop=True),
               s_pe if tc == 1 else None)
        W("sync", s_hid)
        for ho in range(HT):
            pb2 = ho % 2
            DMA("sync", xnt[pb2], d_xnf[ho * 128:(ho + 1) * 128, :], s_a[pb2])
            ld_xn = s_a[pb2].n
            if ho >= 1:
                W("tensor", s_dv, s_dv.n)
            for tc in range(2):
                OP("tensor", lambda t, h0=ho, c0=tc: t.matmul(
                    pA[:, c0 * 512:(c0 + 1) * 512], t_eb[0:E, h0 * 128:(h0 + 1) * 128],
                    snT[0:E, c0 * 512:(c0 + 1) * 512], start=True, stop=True),
                   s_pe if tc == 1 else None)
            if ho > 0:
                W("sync", s_dv, s_dv.n)
            DMA("sync", sld, d_hid[ho * 128:(ho + 1) * 128, :], s_c[0])
            ld_s = s_c[0].n
            W("vector", s_pe, s_pe.n)
            W("vector", s_a[pb2], ld_xn)
            OPW("vector", lambda v, xb=xnt[pb2]: v.tensor_mul(hidw, xb, pB[:, 1024:1024 + TOK]), s_dv)
            OP("vector", lambda v: v.tensor_add(hidw, hidw, pA[:, 0:TOK]), s_dv)
            W("scalar", s_dv, s_dv.n)
            OP("scalar", lambda s: s.activation(gtile, hidw, AF.Silu), s_ac)
            W("vector", s_ac, s_ac.n)
            W("vector", s_c[0], ld_s)
            OP("vector", lambda v, h0=ho: v.tensor_mul(
                sres3[:, h0, :], sld, gtile), s_dv)
        dv_sg_done = s_dv.n

        xtt = [V(142 * KB, H, F32), V(150 * KB, H, F32)]
        W("vector", s_dv, dv_sg_done)
        for tt in range(TOK // 128):
            pb2 = tt % 2
            DMA("sync", xtt[pb2], d_xown[tt * 128:(tt + 1) * 128, :], s_a[pb2])
            ld_x = s_a[pb2].n
            if tt >= 1:
                W("vector", s_st, s_st.n)
            for ho in range(HT):
                W("tensor", s_dv, dv_sg_done if (tt == 0 and ho == 0) else s_dv.n)
                OP("tensor", lambda t, t0=tt, h0=ho: t.transpose(
                    pB[:, 0:128], sres3[:, h0, t0 * 128:(t0 + 1) * 128], t_id32[:]), s_pe)
                W("vector", s_pe, s_pe.n)
                if ho == 0:
                    W("vector", s_a[pb2], ld_x)
                OP("vector", lambda v, h0=ho, xb=xtt[pb2]: v.tensor_add(
                    outw[:, h0 * 128:(h0 + 1) * 128], xb[:, h0 * 128:(h0 + 1) * 128],
                    pB[:, 0:128]), s_dv)
            W("sync", s_dv, s_dv.n)
            DMA("sync", d_out[tt * 128:(tt + 1) * 128, :], outw, s_st)
        W("sync", s_st, s_st.n)

        block = es.enter_context(nc.Block())

        @block.sync
        def _(eng):
            for fn in rec["sync"]:
                fn(eng)

        @block.vector
        def _(eng):
            for fn in rec["vector"]:
                fn(eng)

        @block.scalar
        def _(eng):
            for fn in rec["scalar"]:
                fn(eng)

        @block.tensor
        def _(eng):
            for fn in rec["tensor"]:
                fn(eng)

        @block.gpsimd
        def _(eng):
            for fn in rec["gpsimd"]:
                fn(eng)

    return nc


def _pack_rows(v, ntiles):
    return np.ascontiguousarray(v.reshape(ntiles, 128).T)


def _pack_lhsT(wT, ktiles, cols):
    return np.ascontiguousarray(
        wT.reshape(ktiles, 128, cols).transpose(1, 0, 2).reshape(128, ktiles * cols))


def _pack_tiles_bf16(wT, ktiles, otiles):
    import ml_dtypes
    a = wT.reshape(ktiles, 128, otiles, 128).transpose(1, 2, 0, 3)
    return np.ascontiguousarray(
        a.reshape(128, otiles * ktiles * 128).astype(ml_dtypes.bfloat16))


def kernel(**inputs):
    f = lambda k: np.asarray(inputs[k], dtype=np.float32)
    x = f("L0_attn_hidden_states")
    w_in = f("L0_attn_w_in")
    ln_w = f("L0_attn_input_ln_w")
    conv_w = f("L0_attn_conv_w")
    conv_b = f("L0_attn_conv_b")
    w_x = f("L0_attn_w_x")
    dt_ln = f("L0_attn_dt_ln_w")
    b_ln = f("L0_attn_b_ln_w")
    c_ln = f("L0_attn_c_ln_w")
    w_dt = f("L0_attn_w_dt")
    b_dt = f("L0_attn_b_dt")
    A_log = f("L0_attn_A_log")
    D_skip = f("L0_attn_D_skip")
    w_out = f("L0_attn_w_out")
    ff_ln = f("L0_attn_pre_ff_ln_w")
    w_g = f("L0_attn_w_gate")
    w_u = f("L0_attn_w_up")
    w_d = f("L0_attn_w_down")
    r_w = f("L0_ffn_router_weight")
    r_b = f("L0_ffn_router_bias")
    e_b = f("L0_ffn_expert_bias")

    w_inT = np.ascontiguousarray(w_in.T)
    base = {
        "ln_w": _pack_rows(ln_w, HT),
        "ff_ln_w": _pack_rows(ff_ln, HT),
        "dt_ln_w": np.ascontiguousarray(dt_ln.reshape(128, 1)),
        "b_ln_w": np.ascontiguousarray(b_ln.reshape(16, 1)),
        "c_ln_w": np.ascontiguousarray(c_ln.reshape(16, 1)),
        "w_outT": _pack_tiles_bf16(np.ascontiguousarray(w_out.T), KT, HT),
        "w_gT": _pack_tiles_bf16(np.ascontiguousarray(w_g.T), HT, FT),
        "w_uT": _pack_tiles_bf16(np.ascontiguousarray(w_u.T), HT, FT),
        "w_dT": _pack_tiles_bf16(np.ascontiguousarray(w_d.T), FT, HT),
        "r_wT": _pack_lhsT(np.ascontiguousarray(r_w.T), HT, E),
        "r_b": np.ascontiguousarray(r_b.reshape(E, 1)),
        "e_bias": np.ascontiguousarray(e_b),
    }

    nc = build_program()
    in_maps = []
    for c in range(8):
        b, i = c // 2, c % 2
        dsl = slice(i * DH, (i + 1) * DH)
        m = dict(base)
        m["x_b"] = np.ascontiguousarray(x[b])
        m["x_own"] = np.ascontiguousarray(x[b, i * TOK:(i + 1) * TOK])
        m["parity"] = np.array([[i]], dtype=np.int32)
        m["w_inT_u"] = _pack_tiles_bf16(np.ascontiguousarray(w_inT[:, dsl]), HT, DT)
        m["w_inT_g"] = _pack_tiles_bf16(
            np.ascontiguousarray(w_inT[:, DI + i * DH: DI + (i + 1) * DH]), HT, DT)
        m["conv_w"] = np.ascontiguousarray(
            conv_w[dsl, 0, :].reshape(DT, 128, KC).transpose(1, 0, 2).reshape(128, DT * KC))
        m["conv_b"] = _pack_rows(conv_b[dsl], DT)
        m["w_xT"] = _pack_lhsT(np.ascontiguousarray(w_x.T[dsl]), DT, 160)
        m["w_dtT"] = np.ascontiguousarray(w_dt.T[:, dsl])
        m["b_dt"] = _pack_rows(b_dt[dsl], DT)
        m["A_log"] = np.ascontiguousarray(
            A_log[dsl].reshape(DT, 128, NS).transpose(1, 0, 2).reshape(128, DT * NS))
        m["D_skip"] = _pack_rows(D_skip[dsl], DT)
        in_maps.append(m)

    res = run_bass_kernel_spmd(nc, in_maps, core_ids=list(range(8)))
    out = np.empty((BATCH, S, H), dtype=np.float32)
    for c in range(8):
        b, i = c // 2, c % 2
        out[b, i * TOK:(i + 1) * TOK] = np.asarray(res.results[c]["out"])
    return out


# version 06 (final)
# speedup vs baseline: 1.0096x; 1.0096x over previous
import numpy as np
from contextlib import ExitStack

import concourse.bass as bass
import concourse.mybir as mybir
from concourse.bass_utils import run_bass_kernel_spmd

F32 = mybir.dt.float32
BF16 = mybir.dt.bfloat16
I32 = mybir.dt.int32
U8 = mybir.dt.uint8
AF = mybir.ActivationFunctionType
ALU = mybir.AluOpType
AX = mybir.AxisListType

BATCH, S, H, DI, NS, KC, R, FF, E = 4, 2048, 2048, 4096, 16, 4, 128, 5632, 64
DH = DI // 2
TOK = S // 2
HT = H // 128
DT = DH // 128
FT = FF // 128
KT = DI // 128
TCH = 512
NCH = S // TCH
GROUPS = [[0, 1], [2, 3], [4, 5], [6, 7]]
EPS_LN = 1e-5
EPS_RMS = 1e-6
KB = 1024

ENGS = ("sync", "vector", "scalar", "tensor", "gpsimd")


class Sem:
    def __init__(self, nc, es, name):
        self.h = es.enter_context(nc.semaphore(name))
        self.n = 0


def build_program():
    nc = bass.Bass("TRN2", target_bir_lowering=False)

    d_xb = nc.dram_tensor("x_b", [S, H], F32, kind="ExternalInput").ap()
    d_xown = nc.dram_tensor("x_own", [TOK, H], F32, kind="ExternalInput").ap()
    d_par = nc.dram_tensor("parity", [1, 1], I32, kind="ExternalInput").ap()
    d_winu = nc.dram_tensor("w_inT_u", [128, DT * HT * 128], BF16, kind="ExternalInput").ap()
    d_wing = nc.dram_tensor("w_inT_g", [128, DT * HT * 128], BF16, kind="ExternalInput").ap()
    d_lnw = nc.dram_tensor("ln_w", [128, HT], F32, kind="ExternalInput").ap()
    d_cw = nc.dram_tensor("conv_w", [128, DT * KC], F32, kind="ExternalInput").ap()
    d_cb = nc.dram_tensor("conv_b", [128, DT], F32, kind="ExternalInput").ap()
    d_wx = nc.dram_tensor("w_xT", [128, DT * 160], F32, kind="ExternalInput").ap()
    d_dtln = nc.dram_tensor("dt_ln_w", [128, 1], F32, kind="ExternalInput").ap()
    d_bln = nc.dram_tensor("b_ln_w", [16, 1], F32, kind="ExternalInput").ap()
    d_cln = nc.dram_tensor("c_ln_w", [16, 1], F32, kind="ExternalInput").ap()
    d_wdt = nc.dram_tensor("w_dtT", [R, DH], F32, kind="ExternalInput").ap()
    d_bdt = nc.dram_tensor("b_dt", [128, DT], F32, kind="ExternalInput").ap()
    d_alog = nc.dram_tensor("A_log", [128, DT * NS], F32, kind="ExternalInput").ap()
    d_dsk = nc.dram_tensor("D_skip", [128, DT], F32, kind="ExternalInput").ap()
    d_wout = nc.dram_tensor("w_outT", [128, HT * KT * 128], BF16, kind="ExternalInput").ap()
    d_ffln = nc.dram_tensor("ff_ln_w", [128, HT], F32, kind="ExternalInput").ap()
    d_wg = nc.dram_tensor("w_gT", [128, FT * HT * 128], BF16, kind="ExternalInput").ap()
    d_wu = nc.dram_tensor("w_uT", [128, FT * HT * 128], BF16, kind="ExternalInput").ap()
    d_wd = nc.dram_tensor("w_dT", [128, HT * FT * 128], BF16, kind="ExternalInput").ap()
    d_rw = nc.dram_tensor("r_wT", [128, HT * E], F32, kind="ExternalInput").ap()
    d_rb = nc.dram_tensor("r_b", [E, 1], F32, kind="ExternalInput").ap()
    d_eb = nc.dram_tensor("e_bias", [E, H], F32, kind="ExternalInput").ap()
    d_out = nc.dram_tensor("out", [TOK, H], F32, kind="ExternalOutput").ap()

    d_uf = nc.dram_tensor("u_f32", [DH, S], F32, kind="Internal").ap()
    d_ub = nc.dram_tensor("u_bf", [DH, S], BF16, kind="Internal").ap()
    d_sg = nc.dram_tensor("sg_f32", [DH, S], F32, kind="Internal").ap()
    d_xnf = nc.dram_tensor("xnT_f32", [H, TOK], F32, kind="Internal").ap()
    d_xnb = nc.dram_tensor("xnT_bf", [H, TOK], BF16, kind="Internal").ap()
    d_hid = nc.dram_tensor("hid_f32", [H, TOK], F32, kind="Internal").ap()
    d_ari = [nc.dram_tensor(f"ar_in{j}", [160, TCH], F32, kind="Internal").ap()
             for j in range(NCH)]
    d_aro = [nc.dram_tensor(f"ar_out{j}", [160, TCH], F32, kind="Internal").ap()
             for j in range(NCH)]
    d_agi = nc.dram_tensor("ag_in", [DH, S], BF16, kind="Internal").ap()
    d_ago = nc.dram_tensor("ag_out", [DI, S], BF16, kind="Internal").ap()

    es = ExitStack()
    rec = {e: [] for e in ENGS}

    def EMIT(eng, fn):
        rec[eng].append(fn)

    with es:
        AR_BYTES = 160 * KB
        arena_h = es.enter_context(nc.sbuf_tensor("arena", [128, AR_BYTES], U8))

        def V(off_b, elems, dtype):
            sz = elems * mybir.dt.size(dtype)
            assert off_b % 32 == 0 and off_b + sz <= AR_BYTES, (off_b, sz)
            return arena_h[:, off_b:off_b + sz].bitcast(dtype)

        def sb(name, shape, dtype):
            return es.enter_context(nc.sbuf_tensor(name, shape, dtype))

        t_lnw = sb("lnw", [128, HT], F32)
        t_cw = sb("cw", [128, DT * KC], F32)
        t_cb = sb("cb", [128, DT], F32)
        t_wx = sb("wx", [128, DT * 160], BF16)
        t_dtln = sb("dtln", [128, 1], F32)
        t_bln = sb("bln", [16, 1], F32)
        t_cln = sb("cln", [16, 1], F32)
        t_wdt = sb("wdt", [R, DH], BF16)
        t_bdt = sb("bdt", [128, DT], F32)
        t_aneg = sb("aneg", [128, DT * NS], F32)
        t_dsk = sb("dsk", [128, DT], F32)
        t_ffln = sb("ffln", [128, HT], F32)
        t_rb = sb("rb", [E, 1], F32)
        t_eb = sb("eb", [E, H], BF16)
        t_hst = sb("hstate", [128, DT * NS], F32)
        t_id32 = sb("ident", [128, 128], F32)
        t_idb = sb("identb", [128, 128], BF16)
        t_ones = sb("ones", [128, 128], F32)
        t_oneh = sb("onehot", [16, 16 * 128], F32)
        t_parb = sb("parb", [1, 1], I32)
        t_eps = sb("eps", [128, 2], F32)
        t_sc = sb("scratch_sc", [128, 32], F32)

        pA = es.enter_context(nc.psum_tensor("pA", [128, 2048], F32))
        pB = es.enter_context(nc.psum_tensor("pB", [128, 2048], F32))
        pA_b3_bf = pA[:, 1536:2048].bitcast(BF16)

        s_cc = Sem(nc, es, "s_cc")
        s_st = Sem(nc, es, "s_st")
        s_pe = Sem(nc, es, "s_pe")
        s_dv = Sem(nc, es, "s_dv")
        s_ac = Sem(nc, es, "s_ac")
        s_gp = Sem(nc, es, "s_gp")
        s_lw = Sem(nc, es, "s_lw")
        s_gw = Sem(nc, es, "s_gw")
        s_ar = Sem(nc, es, "s_ar")
        s_yg = Sem(nc, es, "s_yg")
        s_hid = Sem(nc, es, "s_hid")
        s_spA = Sem(nc, es, "s_spA")
        s_uf = Sem(nc, es, "s_uf")
        s_ub = Sem(nc, es, "s_ub")
        s_sg = Sem(nc, es, "s_sg")
        s_a = [Sem(nc, es, "s_a0"), Sem(nc, es, "s_a1")]
        s_b = [Sem(nc, es, "s_b0"), Sem(nc, es, "s_b1")]
        s_c = [Sem(nc, es, "s_c0"), Sem(nc, es, "s_c1")]
        s_w = [Sem(nc, es, "s_w0"), Sem(nc, es, "s_w1")]

        par_reg = es.enter_context(nc.gpsimd.register("par_reg"))

        def W(eng, sem, val=None):
            v = sem.n if val is None else val
            if v > 0:
                EMIT(eng, lambda e, s=sem.h, vv=v: e.wait_ge(s, vv))

        def OP(eng, maker, sem=None, k=1):
            if sem is None:
                EMIT(eng, maker)
                return None
            sem.n += k
            EMIT(eng, lambda e, m=maker, s=sem.h, kk=k: m(e).then_inc(s, kk))
            return sem.n

        def OPW(eng, maker, sem):
            n = OP(eng, maker, sem)
            W(eng, sem, n)
            return n

        def DMA(eng, out_ap, in_ap, sem):
            sem.n += 16
            EMIT(eng, lambda e, o=out_ap, i=in_ap, s=sem.h:
                 e.dma_start(out=o, in_=i).then_inc(s, 16))
            return sem.n

        def CC(kind, op, ins, outs):
            s_cc.n += 1
            EMIT("gpsimd", lambda g, kk=kind, oo=op, ii=ins, uu=outs:
                 g.collective_compute(kk, oo, replica_groups=GROUPS,
                                      ins=ii, outs=uu).then_inc(s_cc.h, 1))
            W("gpsimd", s_cc)

        for ap_sb, ap_dr in [(t_lnw[:], d_lnw), (t_cw[:], d_cw), (t_cb[:], d_cb),
                             (t_dtln[:], d_dtln), (t_bln[:], d_bln), (t_cln[:], d_cln),
                             (t_bdt[:], d_bdt), (t_dsk[:], d_dsk), (t_ffln[:], d_ffln),
                             (t_rb[:], d_rb), (t_parb[:], d_par)]:
            DMA("sync", ap_sb, ap_dr, s_lw)
        DMA("sync", t_aneg[:], d_alog, s_lw)
        ld_alog = s_lw.n
        DMA("gpsimd", t_wx[:], d_wx, s_gw)
        DMA("gpsimd", t_wdt[:], d_wdt, s_gw)
        DMA("gpsimd", t_eb[:], d_eb, s_gw)
        W("gpsimd", s_lw, ld_alog)
        OP("gpsimd", lambda g: g.reg_load(par_reg, t_parb[:1, :1]))

        OP("vector", lambda v: v.memset(t_ones[:], 1.0))
        OP("vector", lambda v: v.memset(t_eps[:, 0:1], EPS_LN))
        OP("vector", lambda v: v.memset(t_eps[:, 1:2], EPS_RMS))
        OP("vector", lambda v: v.memset(t_oneh[:], 0.0))
        OP("vector", lambda v: v.memset(t_hst[:], 0.0), s_dv)
        dv_ones = s_dv.n
        W("gpsimd", s_dv, dv_ones)
        OP("gpsimd", lambda g: g.affine_select(
            out=t_id32[:], in_=t_ones[:], pattern=[[1, 128]],
            compare_op=ALU.is_equal, fill=0.0, base=0, channel_multiplier=-1))
        for n in range(16):
            OP("gpsimd", lambda g, nn=n: g.affine_select(
                out=t_oneh[0:16, nn * 128:(nn + 1) * 128], in_=t_ones[0:16, 0:128],
                pattern=[[0, 128]], compare_op=ALU.is_equal, fill=0.0,
                base=-nn, channel_multiplier=1), s_gp if n == 15 else None)
        gp_init = s_gp.n
        W("vector", s_gp, gp_init)
        OP("vector", lambda v: v.tensor_copy(t_idb[:], t_id32[:]), s_dv)
        W("scalar", s_lw, ld_alog)
        OP("scalar", lambda s: s.activation(t_aneg[:], t_aneg[:], AF.Exp), s_ac)
        W("vector", s_ac, s_ac.n)
        OP("vector", lambda v: v.tensor_scalar_mul(t_aneg[:], t_aneg[:], -1.0), s_dv)
        dv_init = s_dv.n

        xn2T = V(0, HT * S, BF16)
        xa = [V(64 * KB, H, F32), V(72 * KB, H, F32)]
        za = [V(80 * KB, H, BF16), V(84 * KB, H, BF16)]
        zf = [V(88 * KB, H, F32), V(96 * KB, H, F32)]
        sq = V(104 * KB, H, F32)
        trf = V(112 * KB, 128, F32)
        trb = V(113 * KB, 128, BF16)
        wtile = [V(114 * KB, HT * 128, BF16), V(118 * KB, HT * 128, BF16)]
        convb = V(122 * KB, S, F32)
        ufb = V(130 * KB, S, F32)
        ubb = V(138 * KB, S, BF16)
        sgfb = V(142 * KB, S, F32)

        def ln_stats(buf_x, col):
            m = t_sc[:, col:col + 1]
            v_ = t_sc[:, col + 1:col + 2]
            r1 = t_sc[:, col + 2:col + 3]
            r12 = t_sc[:, col + 3:col + 4]
            tmp = t_sc[:, col + 4:col + 5]
            OPW("vector", lambda v: v.reduce_sum(m, buf_x, axis=AX.X), s_dv)
            OPW("vector", lambda v: v.tensor_scalar_mul(m, m, 1.0 / H), s_dv)
            W("scalar", s_dv, s_dv.n)
            OPW("scalar", lambda s: s.activation(sq, buf_x, AF.Square), s_ac)
            W("vector", s_ac, s_ac.n)
            OPW("vector", lambda v: v.reduce_sum(v_, sq, axis=AX.X), s_dv)
            OPW("vector", lambda v: v.tensor_scalar_mul(v_, v_, 1.0 / H), s_dv)
            OPW("vector", lambda v: v.tensor_mul(tmp, m, m), s_dv)
            OPW("vector", lambda v: v.tensor_sub(v_, v_, tmp), s_dv)
            W("scalar", s_dv, s_dv.n)
            OPW("scalar", lambda s: s.activation(tmp, v_, AF.Ln, bias=t_eps[:, 0:1]), s_ac)
            OPW("scalar", lambda s: s.activation(r1, tmp, AF.Exp, scale=-0.5), s_ac)
            W("vector", s_ac, s_ac.n)
            OPW("vector", lambda v: v.tensor_mul(tmp, r1, r1), s_dv)
            OPW("vector", lambda v: v.tensor_mul(tmp, v_, tmp), s_dv)
            W("scalar", s_dv, s_dv.n)
            OPW("scalar", lambda s: s.activation(tmp, tmp, AF.Ln, bias=t_eps[:, 1:2]), s_ac)
            OPW("scalar", lambda s: s.activation(tmp, tmp, AF.Exp, scale=-0.5), s_ac)
            W("vector", s_ac, s_ac.n)
            OPW("vector", lambda v: v.tensor_mul(r12, r1, tmp), s_dv)
            return m, r1, r12

        tile_done_dv = {}
        tile_done_ac = {}
        for tt in range(S // 128):
            pbuf = tt % 2
            if tt >= 2:
                W("sync", s_dv, tile_done_dv[tt - 2])
                W("sync", s_ac, tile_done_ac[tt - 2])
            DMA("sync", xa[pbuf], d_xb[tt * 128:(tt + 1) * 128, :], s_a[pbuf])
            W("vector", s_a[pbuf])
            m, r1, r12 = ln_stats(xa[pbuf], 0 if pbuf == 0 else 8)
            tile_done_ac[tt] = s_ac.n
            OP("vector", lambda v, xb=xa[pbuf], zb=za[pbuf], mm=m, rr=r12:
               v.tensor_scalar(zb, xb, mm, rr, ALU.subtract, ALU.mult), s_dv)
            tile_done_dv[tt] = s_dv.n
            dv_z = s_dv.n
            for ht in range(HT):
                W("tensor", s_dv, dv_z if ht == 0 else s_dv.n)
                OP("tensor", lambda t, zb=za[pbuf], h0=ht:
                   t.transpose(pA_b3_bf[:, 0:128], zb[:, h0 * 128:(h0 + 1) * 128],
                               t_idb[:]), s_pe)
                W("vector", s_pe, s_pe.n)
                OP("vector", lambda v, h0=ht, t0=tt:
                   v.tensor_scalar(xn2T[:, h0 * S + t0 * 128: h0 * S + (t0 + 1) * 128],
                                   pA_b3_bf[:, 0:128],
                                   t_lnw[:, h0:h0 + 1], None, ALU.mult), s_dv)
        dvA_done = s_dv.n

        for tt in range(TOK // 128):
            pbuf = tt % 2
            if tt >= 2:
                W("sync", s_dv, tile_done_dv[100 + tt - 2])
                W("sync", s_ac, tile_done_ac[100 + tt - 2])
            else:
                W("sync", s_dv, tile_done_dv[14 + tt])
                W("sync", s_ac, tile_done_ac[14 + tt])
            DMA("sync", xa[pbuf], d_xown[tt * 128:(tt + 1) * 128, :], s_a[pbuf])
            W("vector", s_a[pbuf])
            m, r1, r12 = ln_stats(xa[pbuf], 0 if pbuf == 0 else 8)
            tile_done_ac[100 + tt] = s_ac.n
            OP("vector", lambda v, xb=xa[pbuf], zb=zf[pbuf], mm=m, rr=r1:
               v.tensor_scalar(zb, xb, mm, rr, ALU.subtract, ALU.mult), s_dv)
            tile_done_dv[100 + tt] = s_dv.n
            dv_z = s_dv.n
            for ht in range(HT):
                W("tensor", s_dv, dv_z if ht == 0 else s_dv.n)
                if ht > 0 or tt > 0:
                    W("tensor", s_spA)
                OP("tensor", lambda t, zb=zf[pbuf], h0=ht:
                   t.transpose(pB[:, 1536:1664], zb[:, h0 * 128:(h0 + 1) * 128],
                               t_id32[:]), s_pe)
                W("vector", s_pe, s_pe.n)
                OP("vector", lambda v: v.tensor_copy(trf, pB[:, 1536:1664]))
                OP("vector", lambda v: v.tensor_copy(trb, pB[:, 1536:1664]), s_dv)
                W("sync", s_dv, s_dv.n)
                DMA("sync", d_xnf[ht * 128:(ht + 1) * 128, tt * 128:(tt + 1) * 128], trf, s_spA)
                DMA("sync", d_xnb[ht * 128:(ht + 1) * 128, tt * 128:(tt + 1) * 128], trb, s_spA)
        dvA2_done = s_dv.n

        winu3 = d_winu.rearrange("p (o x) -> p o x", x=HT * 128)
        wing3 = d_wing.rearrange("p (o x) -> p o x", x=HT * 128)

        pe_grp = {}
        psum_free = {}
        ld_ufsp = {}
        ld_ubsp = {}
        ld_sgsp = {}
        dv_ubcp = {}
        ac_silu = {}

        for o in range(2 * DT):
            is_u = o < DT
            ou = o if is_u else o - DT
            src = winu3 if is_u else wing3
            wb = wtile[o % 2]
            wb3 = wb.rearrange("p (h c) -> p h c", c=128)
            if o >= 2:
                W("gpsimd", s_pe, pe_grp[o - 2])
            DMA("gpsimd", wb, src[:, ou, :], s_w[o % 2])

            pt = pA if (o % 2 == 0) else pB
            W("tensor", s_w[o % 2])
            if o == 0:
                W("tensor", s_dv, dvA_done)
            if o == 1:
                W("tensor", s_dv, dvA2_done)
            if o >= 2:
                sem, val = psum_free[o - 2]
                W("tensor", sem, val)
            for ht in range(HT):
                for tc in range(4):
                    OP("tensor", lambda t, p=pt, h0=ht, c0=tc, w3=wb3:
                       t.matmul(p[:, c0 * 512:(c0 + 1) * 512],
                                w3[:, h0, :],
                                xn2T[:, h0 * S + c0 * 512: h0 * S + (c0 + 1) * 512],
                                start=(h0 == 0), stop=(h0 == HT - 1)),
                       s_pe if (ht == HT - 1 and tc == 3) else None)
            pe_grp[o] = s_pe.n

            if is_u:
                W("vector", s_pe, pe_grp[o])
                prev = o - 1 if o >= 1 else None
                if prev is not None and prev < DT:
                    W("vector", s_ac, ac_silu[prev])
                OPW("vector", lambda v, p=pt, o0=o:
                    v.tensor_scalar(convb[:, 3:S], p[:, 0:S - 3],
                                    t_cw[:, o0 * KC:o0 * KC + 1], None, ALU.mult), s_dv)
                for k in range(1, KC):
                    OPW("vector", lambda v, p=pt, o0=o, kk=k:
                        v.scalar_tensor_tensor(convb[:, 3:S], p[:, kk:kk + S - 3],
                                               t_cw[:, o0 * KC + kk:o0 * KC + kk + 1],
                                               convb[:, 3:S], ALU.mult, ALU.add), s_dv)
                OP("vector", lambda v, p=pt, o0=o:
                   v.tensor_scalar(convb[:, 0:1], p[:, 0:1],
                                   t_cw[:, o0 * KC + 3:o0 * KC + 4], None, ALU.mult))
                OPW("vector", lambda v, p=pt, o0=o:
                    v.tensor_scalar(convb[:, 1:2], p[:, 0:1],
                                    t_cw[:, o0 * KC + 2:o0 * KC + 3], None, ALU.mult), s_dv)
                OP("vector", lambda v, p=pt, o0=o:
                   v.scalar_tensor_tensor(convb[:, 1:2], p[:, 1:2],
                                          t_cw[:, o0 * KC + 3:o0 * KC + 4],
                                          convb[:, 1:2], ALU.mult, ALU.add))
                OPW("vector", lambda v, p=pt, o0=o:
                    v.tensor_scalar(convb[:, 2:3], p[:, 0:1],
                                    t_cw[:, o0 * KC + 1:o0 * KC + 2], None, ALU.mult), s_dv)
                for k2 in (2, 3):
                    OPW("vector", lambda v, p=pt, o0=o, kk=k2:
                        v.scalar_tensor_tensor(convb[:, 2:3], p[:, kk - 1:kk],
                                               t_cw[:, o0 * KC + kk:o0 * KC + kk + 1],
                                               convb[:, 2:3], ALU.mult, ALU.add), s_dv)
                psum_free[o] = (s_dv, s_dv.n)
                W("scalar", s_dv, s_dv.n)
                if o >= 1 and (o - 1) in ld_ufsp:
                    W("scalar", s_dv, dv_ubcp[o - 1])
                    W("scalar", s_uf, ld_ufsp[o - 1])
                OP("scalar", lambda s, o0=o:
                   s.activation(ufb, convb[:, 0:S], AF.Silu,
                                bias=t_cb[:, o0:o0 + 1]), s_ac)
                ac_silu[o] = s_ac.n
                W("vector", s_ac, ac_silu[o])
                if o >= 1 and (o - 1) in ld_ubsp:
                    W("vector", s_ub, ld_ubsp[o - 1])
                OP("vector", lambda v: v.tensor_copy(ubb, ufb), s_dv)
                dv_ubcp[o] = s_dv.n
                W("sync", s_ac, ac_silu[o])
                ld_ufsp[o] = DMA("sync", d_uf[o * 128:(o + 1) * 128, :], ufb, s_uf)
                W("sync", s_dv, dv_ubcp[o])
                ld_ubsp[o] = DMA("sync", d_ub[o * 128:(o + 1) * 128, :], ubb, s_ub)
            else:
                W("scalar", s_pe, pe_grp[o])
                if (o - 1) >= DT:
                    W("scalar", s_sg, ld_sgsp[o - 1])
                OP("scalar", lambda s, p=pt: s.activation(sgfb, p[:, 0:S], AF.Silu), s_ac)
                psum_free[o] = (s_ac, s_ac.n)
                W("sync", s_ac, s_ac.n)
                ld_sgsp[o] = DMA("sync", d_sg[ou * 128:(ou + 1) * 128, :], sgfb, s_sg)
        acB_done = s_ac.n
        dvB_done = s_dv.n

        brep = V(0, NS * TCH, F32)
        crep = V(32 * KB, NS * TCH, F32)
        dab = V(64 * KB, 4 * TCH, F32)
        dbb = V(72 * KB, 4 * TCH, F32)
        htr = V(80 * KB, 4 * TCH, F32)
        dtt = [V(88 * KB, TCH, F32), V(90 * KB, TCH, F32)]
        dtu = V(92 * KB, TCH, F32)
        uft = [V(94 * KB, TCH, F32), V(96 * KB, TCH, F32)]
        ubt = [V(98 * KB, TCH, BF16), V(99 * KB, TCH, BF16)]
        sgt = [V(100 * KB, TCH, F32), V(102 * KB, TCH, F32)]
        ya = V(104 * KB, TCH, F32)
        yt1 = V(106 * KB, TCH, F32)
        yt2 = V(108 * KB, TCH, F32)
        ytw = V(106 * KB, 2 * TCH, F32)
        ybf = [V(110 * KB, TCH, BF16), V(111 * KB, TCH, BF16)]
        trn = V(112 * KB, TCH, BF16)
        trf32 = V(113 * KB, TCH, F32)
        bnf = V(115 * KB, TCH, F32)
        cnf = V(117 * KB, TCH, F32)
        sqc = V(119 * KB, TCH, F32)
        arst = V(121 * KB, 3 * TCH, F32)
        rfs = V(127 * KB, TCH, F32)
        rfs2 = V(129 * KB, TCH, F32)

        wx3 = t_wx[:].rearrange("p (d o) -> p d o", o=160)
        htr3 = htr.rearrange("p (n t) -> p n t", t=TCH)

        W("tensor", s_gw, 48)
        W("vector", s_dv, dv_init)
        W("sync", s_uf)
        W("sync", s_ub)
        W("sync", s_sg)
        W("vector", s_spA)
        W("gpsimd", s_spA)

        pe_wx = {}
        ac_sp_last = acB_done
        dv_scan = {}
        ld_yst = {}
        for j in range(NCH):
            c0 = j * TCH
            for dt_i in range(DT):
                pb = dt_i % 2
                if dt_i >= 2:
                    W("sync", s_pe, pe_wx[dt_i - 2])
                DMA("sync", ubt[pb], d_ub[dt_i * 128:(dt_i + 1) * 128, c0:c0 + TCH], s_a[pb])
                W("tensor", s_a[pb])
                if dt_i == 0:
                    W("tensor", s_ac, ac_sp_last)
                    W("tensor", s_dv, s_dv.n)
                OP("tensor", lambda t, u=ubt[pb], d0=dt_i:
                   t.matmul(pA[:, 0:TCH], wx3[:, d0, 0:128], u,
                            start=(d0 == 0), stop=(d0 == DT - 1)))
                OP("tensor", lambda t, u=ubt[pb], d0=dt_i:
                   t.matmul(pA[0:16, 512:512 + TCH], wx3[:, d0, 128:144], u,
                            start=(d0 == 0), stop=(d0 == DT - 1)))
                pe_wx[dt_i] = OP("tensor", lambda t, u=ubt[pb], d0=dt_i:
                                 t.matmul(pA[0:16, 1024:1024 + TCH], wx3[:, d0, 144:160], u,
                                          start=(d0 == 0), stop=(d0 == DT - 1)), s_pe)
            W("vector", s_pe, s_pe.n)
            if j > 0:
                W("vector", s_ar)
            OP("vector", lambda v: v.tensor_copy(arst[:, 0:TCH], pA[:, 0:TCH]))
            OP("vector", lambda v: v.tensor_copy(arst[0:16, TCH:2 * TCH], pA[0:16, 512:512 + TCH]))
            OP("vector", lambda v: v.tensor_copy(arst[0:16, 2 * TCH:3 * TCH], pA[0:16, 1024:1024 + TCH]), s_dv)
            W("gpsimd", s_dv, s_dv.n)
            DMA("gpsimd", d_ari[j][0:128, :], arst[:, 0:TCH], s_ar)
            DMA("gpsimd", d_ari[j][128:144, :], arst[0:16, TCH:2 * TCH], s_ar)
            DMA("gpsimd", d_ari[j][144:160, :], arst[0:16, 2 * TCH:3 * TCH], s_ar)
            W("gpsimd", s_ar)
            CC("AllReduce", ALU.add, [d_ari[j]], [d_aro[j]])
            DMA("gpsimd", trf32, d_aro[j][0:128, :], s_ar)
            DMA("gpsimd", bnf[0:16, :], d_aro[j][128:144, :], s_ar)
            DMA("gpsimd", cnf[0:16, :], d_aro[j][144:160, :], s_ar)
            gd_ar = s_ar.n

            W("scalar", s_ar, gd_ar)
            W("scalar", s_pe, s_pe.n)
            OP("scalar", lambda s: s.activation(sqc, trf32, AF.Square), s_ac)
            W("tensor", s_ac, s_ac.n)
            W("tensor", s_dv, s_dv.n)
            OP("tensor", lambda t: t.matmul(pB[0:1, 0:TCH], t_ones[:, 0:1], sqc,
                                            start=True, stop=True), s_pe)
            W("scalar", s_pe, s_pe.n)
            OPW("scalar", lambda s: s.activation(rfs2[0:1, :], pB[0:1, 0:TCH],
                                                 AF.Ln, bias=t_eps[0:1, 1:2], scale=1.0 / R), s_ac)
            OP("scalar", lambda s: s.activation(rfs[0:1, :], rfs2[0:1, :],
                                                AF.Exp, scale=-0.5), s_ac)
            W("tensor", s_ac, s_ac.n)
            OP("tensor", lambda t: t.matmul(pB[:, 1024:1024 + TCH], t_ones[0:1, :],
                                            rfs[0:1, :], start=True, stop=True), s_pe)
            W("vector", s_pe, s_pe.n)
            OP("vector", lambda v: v.scalar_tensor_tensor(
                trn, trf32, t_dtln[:, 0:1], pB[:, 1024:1024 + TCH], ALU.mult, ALU.mult), s_dv)
            for buf, w_ap in ((bnf, t_bln), (cnf, t_cln)):
                W("scalar", s_pe, s_pe.n)
                OP("scalar", lambda s, b=buf: s.activation(sqc[0:16, :], b[0:16, :], AF.Square), s_ac)
                W("tensor", s_ac, s_ac.n)
                OP("tensor", lambda t: t.matmul(pB[0:1, 0:TCH], t_ones[0:16, 0:1], sqc[0:16, :],
                                                start=True, stop=True), s_pe)
                W("scalar", s_pe, s_pe.n)
                OPW("scalar", lambda s: s.activation(rfs2[0:1, :], pB[0:1, 0:TCH],
                                                     AF.Ln, bias=t_eps[0:1, 1:2], scale=1.0 / NS), s_ac)
                OP("scalar", lambda s: s.activation(rfs[0:1, :], rfs2[0:1, :],
                                                    AF.Exp, scale=-0.5), s_ac)
                W("tensor", s_ac, s_ac.n)
                W("tensor", s_dv, s_dv.n)
                OP("tensor", lambda t: t.matmul(pB[0:16, 1024:1024 + TCH], t_ones[0:1, 0:16],
                                                rfs[0:1, :], start=True, stop=True), s_pe)
                W("vector", s_pe, s_pe.n)
                OP("vector", lambda v, b=buf, w=w_ap: v.scalar_tensor_tensor(
                    b[0:16, :], b[0:16, :], w[0:16, 0:1], pB[0:16, 1024:1024 + TCH],
                    ALU.mult, ALU.mult), s_dv)

            for n in range(NS):
                for src_buf, dst in ((bnf, brep), (cnf, crep)):
                    W("tensor", s_dv, s_dv.n)
                    OP("tensor", lambda t, nn=n, sbuf=src_buf: t.matmul(
                        pB[:, 0:TCH], t_oneh[0:16, nn * 128:(nn + 1) * 128], sbuf[0:16, :],
                        start=True, stop=True), s_pe)
                    W("vector", s_pe, s_pe.n)
                    OP("vector", lambda v, nn=n, db=dst: v.tensor_copy(
                        db[:, nn * TCH:(nn + 1) * TCH], pB[:, 0:TCH]), s_dv)
            dv_rep = s_dv.n

            for dt_i in range(DT):
                pb = dt_i % 2
                if dt_i >= 2 or j > 0:
                    W("sync", s_dv, dv_scan[(dt_i - 2) % DT])
                    W("sync", s_c[pb], ld_yst[(dt_i - 2) % DT])
                DMA("sync", uft[pb], d_uf[dt_i * 128:(dt_i + 1) * 128, c0:c0 + TCH], s_b[pb])
                DMA("sync", sgt[pb], d_sg[dt_i * 128:(dt_i + 1) * 128, c0:c0 + TCH], s_b[pb])
                ld_u = s_b[pb].n
                W("tensor", s_dv, dv_rep if dt_i == 0 else s_dv.n)
                W("tensor", s_ac, s_ac.n)
                OP("tensor", lambda t, d0=dt_i: t.matmul(
                    pA[:, 0:TCH], t_wdt[:, d0 * 128:(d0 + 1) * 128], trn,
                    start=True, stop=True), s_pe)
                W("scalar", s_pe, s_pe.n)
                dtb = dtt[pb]
                OPW("scalar", lambda s, d0=dt_i: s.activation(
                    pA[:, 512:512 + TCH], pA[:, 0:TCH], AF.Abs, bias=t_bdt[:, d0:d0 + 1]), s_ac)
                OPW("scalar", lambda s: s.activation(
                    pA[:, 512:512 + TCH], pA[:, 512:512 + TCH], AF.Exp, scale=-1.0), s_ac)
                OP("scalar", lambda s: s.activation(
                    pA[:, 512:512 + TCH], pA[:, 512:512 + TCH], AF.Ln, bias=1.0))
                OP("scalar", lambda s, d0=dt_i, ob=dtb: s.activation(
                    ob, pA[:, 0:TCH], AF.Relu, bias=t_bdt[:, d0:d0 + 1]), s_ac)
                ac_sp_last = s_ac.n
                W("vector", s_ac, s_ac.n)
                OPW("vector", lambda v, ob=dtb: v.tensor_add(ob, ob, pA[:, 512:512 + TCH]), s_dv)
                W("vector", s_b[pb], ld_u)
                OP("vector", lambda v, ob=dtb, u=uft[pb]: v.tensor_mul(dtu, ob, u), s_dv)
                dv_dtu = s_dv.n
                W("scalar", s_dv, dv_dtu)
                for g in range(4):
                    W("scalar", s_dv, s_dv.n)
                    for ni in range(4):
                        n = g * 4 + ni
                        OP("scalar", lambda s, d0=dt_i, nn=n, ni_=ni, ob=dtb: s.activation(
                            dab[:, ni_ * TCH:(ni_ + 1) * TCH], ob, AF.Exp,
                            scale=t_aneg[:, d0 * NS + nn:d0 * NS + nn + 1]),
                           s_ac if ni == 3 else None)
                    W("vector", s_ac, s_ac.n)
                    if g == 0:
                        W("vector", s_dv, dv_dtu)
                    for ni in range(4):
                        n = g * 4 + ni
                        OP("vector", lambda v, nn=n, ni_=ni: v.tensor_mul(
                            dbb[:, ni_ * TCH:(ni_ + 1) * TCH], dtu,
                            brep[:, nn * TCH:(nn + 1) * TCH]), s_dv if ni == 3 else None)
                    W("vector", s_dv, s_dv.n)
                    for ni in range(4):
                        n = g * 4 + ni
                        OP("vector", lambda v, d0=dt_i, nn=n, ni_=ni: v.tensor_tensor_scan(
                            out=htr[:, ni_ * TCH:(ni_ + 1) * TCH],
                            data0=dab[:, ni_ * TCH:(ni_ + 1) * TCH],
                            data1=dbb[:, ni_ * TCH:(ni_ + 1) * TCH],
                            initial=t_hst[:, d0 * NS + nn:d0 * NS + nn + 1],
                            op0=ALU.mult, op1=ALU.add), s_dv if ni == 3 else None)
                    W("vector", s_dv, s_dv.n)
                    OP("vector", lambda v, g0=g: v.tensor_mul(
                        dbb[:, 0:4 * TCH], htr[:, 0:4 * TCH],
                        crep[:, g0 * 4 * TCH:(g0 + 1) * 4 * TCH]), s_dv)
                    OPW("vector", lambda v, d0=dt_i, g0=g: v.tensor_copy(
                        t_hst[:, d0 * NS + g0 * 4: d0 * NS + (g0 + 1) * 4]
                        .rearrange("p (a b) -> p a b", b=1),
                        htr3[:, :, TCH - 1:TCH]), s_dv)
                    OPW("vector", lambda v: v.tensor_add(
                        ytw, dbb[:, 0:2 * TCH], dbb[:, 2 * TCH:4 * TCH]), s_dv)
                    if g == 0:
                        OP("vector", lambda v: v.tensor_add(ya, yt1, yt2), s_dv)
                    else:
                        OPW("vector", lambda v: v.tensor_add(yt1, yt1, yt2), s_dv)
                        OP("vector", lambda v: v.tensor_add(ya, ya, yt1), s_dv)
                W("vector", s_dv, s_dv.n)
                OPW("vector", lambda v, d0=dt_i, u=uft[pb]: v.scalar_tensor_tensor(
                    ya, u, t_dsk[:, d0:d0 + 1], ya, ALU.mult, ALU.add), s_dv)
                OP("vector", lambda v, sgb=sgt[pb], yb=ybf[pb]: v.tensor_mul(yb, ya, sgb), s_dv)
                dv_scan[dt_i] = s_dv.n
                W("sync", s_dv, s_dv.n)
                ld_yst[dt_i] = DMA("sync", d_agi[dt_i * 128:(dt_i + 1) * 128, c0:c0 + TCH],
                                   ybf[pb], s_c[pb])

        W("gpsimd", s_c[0])
        W("gpsimd", s_c[1])
        CC("AllGather", ALU.bypass, [d_agi], [d_ago])

        yag = V(0, KT * TOK, BF16)
        hff = V(0, HT * TOK, BF16)
        sres = V(0, HT * TOK, F32)
        wdl = [V(64 * KB, KT * 128, BF16), V(72 * KB, KT * 128, BF16)]
        wgl = [V(64 * KB, HT * 128, BF16), V(68 * KB, HT * 128, BF16)]
        wul = [V(72 * KB, HT * 128, BF16), V(76 * KB, HT * 128, BF16)]
        wdn = [V(64 * KB, FT * 128, BF16), V(75 * KB, FT * 128, BF16)]
        xnt = [V(86 * KB, TOK, F32), V(90 * KB, TOK, F32)]
        hidw = V(94 * KB, TOK, F32)
        sqd = V(98 * KB, TOK, F32)
        ssum = V(102 * KB, TOK, F32)
        rfacb = V(106 * KB, TOK, F32)
        ffm = V(110 * KB, FT * 512, BF16)
        xnb2 = [V(110 * KB, TOK, BF16), V(112 * KB, TOK, BF16)]
        lg = V(114 * KB, TOK, F32)
        lgt = V(118 * KB, 64, F32)
        m8b = V(119 * KB, 8, F32)
        mskb = V(120 * KB, 64, F32)
        snb = V(121 * KB, 64, BF16)
        snT = V(122 * KB, TOK, BF16)
        swb = V(124 * KB, TOK, F32)
        rwl = V(128 * KB, HT * E, BF16)
        gtile = V(130 * KB, TOK, F32)
        outw = V(134 * KB, H, F32)
        sgate = V(154 * KB, 512, F32)
        sld = V(156 * KB, TOK, F32)

        yag3 = yag.rearrange("p (k t) -> p k t", t=TOK)
        hff3 = hff.rearrange("p (h t) -> p h t", t=TOK)
        d3_dv = {}
        sres3 = sres.rearrange("p (h t) -> p h t", t=TOK)
        ffm3 = ffm.rearrange("p (f t) -> p f t", t=512)

        s_yg.n += 16 * KT

        def _readback(g):
            with g.If_eq(par_reg, 0):
                for k in range(KT):
                    g.dma_start(out=yag3[:, k, :],
                                in_=d_ago[k * 128:(k + 1) * 128, 0:TOK]).then_inc(s_yg.h, 16)
            with g.Else():
                for k in range(KT):
                    g.dma_start(out=yag3[:, k, :],
                                in_=d_ago[k * 128:(k + 1) * 128, TOK:S]).then_inc(s_yg.h, 16)
        EMIT("gpsimd", _readback)
        gd_yag = s_yg.n

        wout3 = d_wout.rearrange("p (t x) -> p t x", x=KT * 128)
        wg3 = d_wg.rearrange("p (t x) -> p t x", x=HT * 128)
        wu3 = d_wu.rearrange("p (t x) -> p t x", x=HT * 128)
        wd3 = d_wd.rearrange("p (t x) -> p t x", x=FT * 128)

        W("sync", s_spA)
        pe_wo = {}
        dv_hid = {}
        for ho in range(HT):
            pb2 = ho % 2
            if ho >= 2:
                W("gpsimd", s_pe, pe_wo[ho - 2])
            DMA("gpsimd", wdl[pb2], wout3[:, ho, :], s_w[pb2])
            W("tensor", s_w[pb2])
            if ho == 0:
                W("tensor", s_yg, gd_yag)
                W("tensor", s_dv, s_dv.n)
                W("tensor", s_ac, s_ac.n)
            if ho >= 2:
                W("tensor", s_dv, dv_hid[ho - 2])
            wv = wdl[pb2].rearrange("p (k c) -> p k c", c=128)
            pg = pA if pb2 == 0 else pB
            for k in range(KT):
                for tc in range(2):
                    OP("tensor", lambda t, p=pg, k0=k, c0=tc, w3=wv:
                       t.matmul(p[:, c0 * 512:(c0 + 1) * 512], w3[:, k0, :],
                                yag3[:, k0, c0 * 512:(c0 + 1) * 512],
                                start=(k0 == 0), stop=(k0 == KT - 1)),
                       s_pe if (k == KT - 1 and tc == 1) else None)
            pe_wo[ho] = s_pe.n
            DMA("sync", xnt[pb2], d_xnf[ho * 128:(ho + 1) * 128, :], s_a[pb2])
            W("vector", s_pe, pe_wo[ho])
            W("vector", s_a[pb2])
            OP("vector", lambda v, p=pg, xb=xnt[pb2]:
               v.tensor_add(hidw, xb, p[:, 0:TOK]), s_dv)
            dv_hid[ho] = s_dv.n
            W("sync", s_dv, s_dv.n)
            DMA("sync", d_hid[ho * 128:(ho + 1) * 128, :], hidw, s_hid)
            W("vector", s_hid)

        W("sync", s_hid)
        d2_ac = {}
        for ho in range(HT):
            pb2 = ho % 2
            if ho >= 2:
                W("sync", s_ac, d2_ac[ho - 2])
            DMA("sync", xnt[pb2], d_hid[ho * 128:(ho + 1) * 128, :], s_a[pb2])
            W("scalar", s_a[pb2])
            if ho >= 1:
                W("scalar", s_pe, s_pe.n)
            OP("scalar", lambda s, xb=xnt[pb2]: s.activation(sqd, xb, AF.Square), s_ac)
            d2_ac[ho] = s_ac.n
            W("tensor", s_ac, s_ac.n)
            if ho == 0:
                W("tensor", s_dv, s_dv.n)
            for tc in range(2):
                OP("tensor", lambda t, c0=tc, h0=ho: t.matmul(
                    pA[0:1, c0 * 512:(c0 + 1) * 512], t_ones[:, 0:1],
                    sqd[:, c0 * 512:(c0 + 1) * 512],
                    start=(h0 == 0), stop=(h0 == HT - 1)),
                   s_pe if tc == 1 else None)
        W("scalar", s_pe, s_pe.n)
        OPW("scalar", lambda s: s.activation(ssum[0:1, :], pA[0:1, 0:TOK],
                                             AF.Ln, bias=t_eps[0:1, 1:2], scale=1.0 / H), s_ac)
        OP("scalar", lambda s: s.activation(ssum[0:1, :], ssum[0:1, :],
                                            AF.Exp, scale=-0.5), s_ac)
        W("tensor", s_ac, s_ac.n)
        for tc in range(2):
            OP("tensor", lambda t, c0=tc: t.matmul(
                pA[:, 1024 + c0 * 512:1024 + (c0 + 1) * 512], t_ones[0:1, :],
                ssum[0:1, c0 * 512:(c0 + 1) * 512], start=True, stop=True),
               s_pe if tc == 1 else None)
        W("vector", s_pe, s_pe.n)
        OP("vector", lambda v: v.tensor_copy(rfacb, pA[:, 1024:1024 + TOK]), s_dv)
        dv_rfac = s_dv.n

        W("sync", s_ac)
        for ho in range(HT):
            pb2 = ho % 2
            if ho >= 2:
                W("sync", s_dv, d3_dv[ho - 2])
            DMA("sync", xnt[pb2], d_hid[ho * 128:(ho + 1) * 128, :], s_a[pb2])
            W("vector", s_a[pb2])
            W("vector", s_dv, dv_rfac)
            OP("vector", lambda v, xb=xnt[pb2], h0=ho: v.scalar_tensor_tensor(
                hff3[:, h0, :], xb, t_ffln[:, h0:h0 + 1], rfacb, ALU.mult, ALU.mult), s_dv)
            d3_dv[ho] = s_dv.n
        dv_hff = s_dv.n

        W("sync", s_dv)
        pe_ff = {}
        dv_ffm = {}
        for th in range(2):
            tc0 = th * 512
            for f in range(FT):
                pb2 = f % 2
                key = (th, f)
                prevk = (th, f - 2) if f >= 2 else None
                if prevk in pe_ff:
                    W("gpsimd", s_pe, pe_ff[prevk])
                elif f < 2:
                    W("gpsimd", s_pe, s_pe.n)
                DMA("gpsimd", wgl[pb2], wg3[:, f, :], s_w[pb2])
                DMA("gpsimd", wul[pb2], wu3[:, f, :], s_w[pb2])
                W("tensor", s_w[pb2])
                if th == 0 and f == 0:
                    W("tensor", s_dv, dv_hff)
                if prevk in dv_ffm:
                    W("tensor", s_dv, dv_ffm[prevk])
                elif th == 1 and f < 2:
                    W("tensor", s_dv, s_dv.n)
                wgv = wgl[pb2].rearrange("p (h c) -> p h c", c=128)
                wuv = wul[pb2].rearrange("p (h c) -> p h c", c=128)
                pg = pA if pb2 == 0 else pB
                for ho in range(HT):
                    OP("tensor", lambda t, p=pg, h0=ho, w3=wgv, c0=tc0:
                       t.matmul(p[:, 0:512], w3[:, h0, :], hff3[:, h0, c0:c0 + 512],
                                start=(h0 == 0), stop=(h0 == HT - 1)))
                    OP("tensor", lambda t, p=pg, h0=ho, w3=wuv, c0=tc0:
                       t.matmul(p[:, 512:1024], w3[:, h0, :], hff3[:, h0, c0:c0 + 512],
                                start=(h0 == 0), stop=(h0 == HT - 1)),
                       s_pe if ho == HT - 1 else None)
                pe_ff[key] = s_pe.n
                W("scalar", s_pe, pe_ff[key])
                if not (th == 0 and f == 0):
                    W("scalar", s_dv, s_dv.n)
                OP("scalar", lambda s, p=pg: s.activation(sgate, p[:, 0:512], AF.Silu), s_ac)
                W("vector", s_ac, s_ac.n)
                OP("vector", lambda v, p=pg, f0=f: v.tensor_mul(
                    ffm3[:, f0, :], p[:, 512:1024], sgate), s_dv)
                dv_ffm[key] = s_dv.n
            for ho in range(HT):
                pb2 = ho % 2
                W("gpsimd", s_pe, s_pe.n)
                DMA("gpsimd", wdn[pb2], wd3[:, ho, :], s_w[pb2])
                W("tensor", s_w[pb2])
                W("tensor", s_dv, s_dv.n)
                wdv = wdn[pb2].rearrange("p (k c) -> p k c", c=128)
                pg = pA if pb2 == 0 else pB
                for k in range(FT):
                    OP("tensor", lambda t, p=pg, k0=k, w3=wdv:
                       t.matmul(p[:, 1536:2048], w3[:, k0, :], ffm3[:, k0, :],
                                start=(k0 == 0), stop=(k0 == FT - 1)),
                       s_pe if k == FT - 1 else None)
                DMA("sync", xnt[pb2][:, 0:512], d_hid[ho * 128:(ho + 1) * 128, tc0:tc0 + 512], s_a[pb2])
                W("vector", s_pe, s_pe.n)
                W("vector", s_a[pb2])
                OP("vector", lambda v, p=pg, xb=xnt[pb2]: v.tensor_add(
                    hidw[:, 0:512], xb[:, 0:512], p[:, 1536:2048]), s_dv)
                W("sync", s_dv, s_dv.n)
                DMA("sync", d_hid[ho * 128:(ho + 1) * 128, tc0:tc0 + 512], hidw[:, 0:512], s_hid)
                W("vector", s_hid)
        dv_sres = s_dv.n

        rw3v = rwl.rearrange("p (h c) -> p h c", c=E)
        W("gpsimd", s_dv, dv_ffm[(1, FT - 1)])
        W("gpsimd", s_pe, s_pe.n)
        DMA("gpsimd", rw3v, d_rw.rearrange("p (h c) -> p h c", c=E), s_gw)
        gd_rw = s_gw.n
        rt_pe = {}
        for ho in range(HT):
            pb2 = ho % 2
            if ho == 0:
                W("sync", s_pe, s_pe.n)
            if ho >= 2:
                W("sync", s_pe, rt_pe[ho - 2])
            DMA("sync", xnb2[pb2], d_xnb[ho * 128:(ho + 1) * 128, :], s_b[pb2])
            W("tensor", s_b[pb2])
            if ho == 0:
                W("tensor", s_gw, gd_rw)
                W("tensor", s_dv, dv_sres)
            for tc in range(2):
                OP("tensor", lambda t, h0=ho, c0=tc, xb=xnb2[pb2]: t.matmul(
                    pA[0:E, c0 * 512:(c0 + 1) * 512], rw3v[:, h0, :],
                    xb[:, c0 * 512:(c0 + 1) * 512],
                    start=(h0 == 0), stop=(h0 == HT - 1)),
                   s_pe if tc == 1 else None)
            rt_pe[ho] = s_pe.n
        W("scalar", s_pe, s_pe.n)
        OP("scalar", lambda s: s.activation(lg[0:E, :], pA[0:E, 0:TOK], AF.Identity,
                                            bias=t_rb[0:E, 0:1]), s_ac)
        ac_lg = s_ac.n
        for tt in range(TOK // 128):
            W("tensor", s_ac, ac_lg if tt == 0 else s_ac.n)
            W("tensor", s_dv, s_dv.n)
            OP("tensor", lambda t, t0=tt: t.transpose(
                pB[:, 0:64], lg[0:E, t0 * 128:(t0 + 1) * 128], t_id32[0:E, 0:E]), s_pe)
            W("vector", s_pe, s_pe.n)
            OPW("vector", lambda v: v.tensor_copy(lgt, pB[:, 0:64]), s_dv)
            OPW("vector", lambda v: v.reduce_max(t_sc[:, 20:21], lgt, axis=AX.X, negate=True), s_dv)
            W("scalar", s_dv, s_dv.n)
            OPW("scalar", lambda s: s.activation(lgt, lgt, AF.Exp, bias=t_sc[:, 20:21]), s_ac)
            W("vector", s_ac, s_ac.n)
            OPW("vector", lambda v: v.max(out=m8b[:, 0:8], in_=lgt), s_dv)
            OPW("vector", lambda v: v.tensor_scalar(mskb, lgt, m8b[:, 7:8], None, ALU.is_ge), s_dv)
            OPW("vector", lambda v: v.reduce_sum(t_sc[:, 21:22], lgt, axis=AX.X), s_dv)
            OPW("vector", lambda v: v.reduce_sum(t_sc[:, 22:23], m8b[:, 0:8], axis=AX.X), s_dv)
            OPW("vector", lambda v: v.reciprocal(t_sc[:, 23:24], t_sc[:, 21:22]), s_dv)
            OPW("vector", lambda v: v.tensor_mul(mskb, mskb, lgt), s_dv)
            OPW("vector", lambda v: v.tensor_scalar(snb, mskb, t_sc[:, 23:24], None, ALU.mult), s_dv)
            OPW("vector", lambda v: v.tensor_mul(t_sc[:, 24:25], t_sc[:, 22:23], t_sc[:, 23:24]), s_dv)
            W("tensor", s_dv, s_dv.n)
            OP("tensor", lambda t: t.transpose(
                pB[:, 512:640].bitcast(BF16)[0:E, 0:128], snb, t_idb[:]), s_pe)
            OP("tensor", lambda t: t.transpose(
                pB[0:1, 1024:1152], t_sc[:, 24:25], t_id32[:]), s_pe)
            W("vector", s_pe, s_pe.n)
            OP("vector", lambda v, t0=tt: v.tensor_copy(
                snT[0:E, t0 * 128:(t0 + 1) * 128], pB[:, 512:640].bitcast(BF16)[0:E, 0:128]))
            OP("vector", lambda v, t0=tt: v.tensor_copy(
                swb[0:1, t0 * 128:(t0 + 1) * 128], pB[0:1, 1024:1152]), s_dv)
        dv_sn = s_dv.n
        W("tensor", s_dv, dv_sn)
        for tc in range(2):
            OP("tensor", lambda t, c0=tc: t.matmul(
                pB[:, 1024 + c0 * 512:1024 + (c0 + 1) * 512], t_ones[0:1, :],
                swb[0:1, c0 * 512:(c0 + 1) * 512], start=True, stop=True),
               s_pe if tc == 1 else None)
        W("sync", s_hid)
        for ho in range(HT):
            pb2 = ho % 2
            DMA("sync", xnt[pb2], d_xnf[ho * 128:(ho + 1) * 128, :], s_a[pb2])
            ld_xn = s_a[pb2].n
            if ho >= 1:
                W("tensor", s_dv, s_dv.n)
            for tc in range(2):
                OP("tensor", lambda t, h0=ho, c0=tc: t.matmul(
                    pA[:, c0 * 512:(c0 + 1) * 512], t_eb[0:E, h0 * 128:(h0 + 1) * 128],
                    snT[0:E, c0 * 512:(c0 + 1) * 512], start=True, stop=True),
                   s_pe if tc == 1 else None)
            if ho > 0:
                W("sync", s_dv, s_dv.n)
            DMA("sync", sld, d_hid[ho * 128:(ho + 1) * 128, :], s_c[0])
            ld_s = s_c[0].n
            W("vector", s_pe, s_pe.n)
            W("vector", s_a[pb2], ld_xn)
            OPW("vector", lambda v, xb=xnt[pb2]: v.tensor_mul(hidw, xb, pB[:, 1024:1024 + TOK]), s_dv)
            OP("vector", lambda v: v.tensor_add(hidw, hidw, pA[:, 0:TOK]), s_dv)
            W("scalar", s_dv, s_dv.n)
            OP("scalar", lambda s: s.activation(gtile, hidw, AF.Silu), s_ac)
            W("vector", s_ac, s_ac.n)
            W("vector", s_c[0], ld_s)
            OP("vector", lambda v, h0=ho: v.tensor_mul(
                sres3[:, h0, :], sld, gtile), s_dv)
        dv_sg_done = s_dv.n

        xtt = [V(142 * KB, H, F32), V(150 * KB, H, F32)]
        W("vector", s_dv, dv_sg_done)
        for tt in range(TOK // 128):
            pb2 = tt % 2
            DMA("sync", xtt[pb2], d_xown[tt * 128:(tt + 1) * 128, :], s_a[pb2])
            ld_x = s_a[pb2].n
            if tt >= 1:
                W("vector", s_st, s_st.n)
            for ho in range(HT):
                W("tensor", s_dv, dv_sg_done if (tt == 0 and ho == 0) else s_dv.n)
                OP("tensor", lambda t, t0=tt, h0=ho: t.transpose(
                    pB[:, 0:128], sres3[:, h0, t0 * 128:(t0 + 1) * 128], t_id32[:]), s_pe)
                W("vector", s_pe, s_pe.n)
                if ho == 0:
                    W("vector", s_a[pb2], ld_x)
                OP("vector", lambda v, h0=ho, xb=xtt[pb2]: v.tensor_add(
                    outw[:, h0 * 128:(h0 + 1) * 128], xb[:, h0 * 128:(h0 + 1) * 128],
                    pB[:, 0:128]), s_dv)
            W("sync", s_dv, s_dv.n)
            DMA("sync", d_out[tt * 128:(tt + 1) * 128, :], outw, s_st)
        W("sync", s_st, s_st.n)

        block = es.enter_context(nc.Block())

        @block.sync
        def _(eng):
            for fn in rec["sync"]:
                fn(eng)

        @block.vector
        def _(eng):
            for fn in rec["vector"]:
                fn(eng)

        @block.scalar
        def _(eng):
            for fn in rec["scalar"]:
                fn(eng)

        @block.tensor
        def _(eng):
            for fn in rec["tensor"]:
                fn(eng)

        @block.gpsimd
        def _(eng):
            for fn in rec["gpsimd"]:
                fn(eng)

    return nc


def _pack_rows(v, ntiles):
    return np.ascontiguousarray(v.reshape(ntiles, 128).T)


def _pack_lhsT(wT, ktiles, cols):
    return np.ascontiguousarray(
        wT.reshape(ktiles, 128, cols).transpose(1, 0, 2).reshape(128, ktiles * cols))


def _pack_tiles_bf16(wT, ktiles, otiles):
    import ml_dtypes
    a = wT.reshape(ktiles, 128, otiles, 128).transpose(1, 2, 0, 3)
    return np.ascontiguousarray(
        a.reshape(128, otiles * ktiles * 128).astype(ml_dtypes.bfloat16))


def kernel(**inputs):
    f = lambda k: np.asarray(inputs[k], dtype=np.float32)
    x = f("L0_attn_hidden_states")
    w_in = f("L0_attn_w_in")
    ln_w = f("L0_attn_input_ln_w")
    conv_w = f("L0_attn_conv_w")
    conv_b = f("L0_attn_conv_b")
    w_x = f("L0_attn_w_x")
    dt_ln = f("L0_attn_dt_ln_w")
    b_ln = f("L0_attn_b_ln_w")
    c_ln = f("L0_attn_c_ln_w")
    w_dt = f("L0_attn_w_dt")
    b_dt = f("L0_attn_b_dt")
    A_log = f("L0_attn_A_log")
    D_skip = f("L0_attn_D_skip")
    w_out = f("L0_attn_w_out")
    ff_ln = f("L0_attn_pre_ff_ln_w")
    w_g = f("L0_attn_w_gate")
    w_u = f("L0_attn_w_up")
    w_d = f("L0_attn_w_down")
    r_w = f("L0_ffn_router_weight")
    r_b = f("L0_ffn_router_bias")
    e_b = f("L0_ffn_expert_bias")

    w_inT = np.ascontiguousarray(w_in.T)
    base = {
        "ln_w": _pack_rows(ln_w, HT),
        "ff_ln_w": _pack_rows(ff_ln, HT),
        "dt_ln_w": np.ascontiguousarray(dt_ln.reshape(128, 1)),
        "b_ln_w": np.ascontiguousarray(b_ln.reshape(16, 1)),
        "c_ln_w": np.ascontiguousarray(c_ln.reshape(16, 1)),
        "w_outT": _pack_tiles_bf16(np.ascontiguousarray(w_out.T), KT, HT),
        "w_gT": _pack_tiles_bf16(np.ascontiguousarray(w_g.T), HT, FT),
        "w_uT": _pack_tiles_bf16(np.ascontiguousarray(w_u.T), HT, FT),
        "w_dT": _pack_tiles_bf16(np.ascontiguousarray(w_d.T), FT, HT),
        "r_wT": _pack_lhsT(np.ascontiguousarray(r_w.T), HT, E),
        "r_b": np.ascontiguousarray(r_b.reshape(E, 1)),
        "e_bias": np.ascontiguousarray(e_b),
    }

    nc = build_program()
    in_maps = []
    for c in range(8):
        b, i = c // 2, c % 2
        dsl = slice(i * DH, (i + 1) * DH)
        m = dict(base)
        m["x_b"] = np.ascontiguousarray(x[b])
        m["x_own"] = np.ascontiguousarray(x[b, i * TOK:(i + 1) * TOK])
        m["parity"] = np.array([[i]], dtype=np.int32)
        m["w_inT_u"] = _pack_tiles_bf16(np.ascontiguousarray(w_inT[:, dsl]), HT, DT)
        m["w_inT_g"] = _pack_tiles_bf16(
            np.ascontiguousarray(w_inT[:, DI + i * DH: DI + (i + 1) * DH]), HT, DT)
        m["conv_w"] = np.ascontiguousarray(
            conv_w[dsl, 0, :].reshape(DT, 128, KC).transpose(1, 0, 2).reshape(128, DT * KC))
        m["conv_b"] = _pack_rows(conv_b[dsl], DT)
        m["w_xT"] = _pack_lhsT(np.ascontiguousarray(w_x.T[dsl]), DT, 160)
        m["w_dtT"] = np.ascontiguousarray(w_dt.T[:, dsl])
        m["b_dt"] = _pack_rows(b_dt[dsl], DT)
        m["A_log"] = np.ascontiguousarray(
            A_log[dsl].reshape(DT, 128, NS).transpose(1, 0, 2).reshape(128, DT * NS))
        m["D_skip"] = _pack_rows(D_skip[dsl], DT)
        in_maps.append(m)

    res = run_bass_kernel_spmd(nc, in_maps, core_ids=list(range(8)))
    out = np.empty((BATCH, S, H), dtype=np.float32)
    for c in range(8):
        b, i = c // 2, c % 2
        out[b, i * TOK:(i + 1) * TOK] = np.asarray(res.results[c]["out"])
    return out
